# version 01
# baseline (speedup 1.0000x reference)
import numpy as np
import ml_dtypes
import concourse.bass as bass
import concourse.mybir as mybir
from concourse.bass_utils import run_bass_kernel_spmd

F32 = mybir.dt.float32
BF16 = mybir.dt.bfloat16
AF = mybir.ActivationFunctionType
ALU = mybir.AluOpType
AX = mybir.AxisListType.X

BC = 2
T = 2048
D = 2048
NT = 16
NH = 16
HD = 128
MOE_I = 1408
NIT = 11
SH_I = 5632
N_EXP = 8
TC = 512
NCH = T // TC
EPS_LN = 1e-5
EPS_RMS = 1e-6
NEG_BIG = -3.0e37

PH_ALL = ["A1", "A2", "A3", "A4", "B2", "B3", "B4", "B5", "B6"]
_BUILD_PHASES = list(PH_ALL)
_OUT_STAGE = "x3"
_LAST_ASM = None


class Asm:
    def __init__(self):
        self.steps = []
        self.counts = {}

    def add_sem(self, name):
        self.counts[name] = 0

    def step(self, eng, fn, waits=(), incs=(), drain=False, tag=""):
        for s, n in incs:
            self.counts[s] += n
        self.steps.append((eng, fn, tuple(w for w in waits if w is not None),
                           tuple(incs), drain, tag))

    def tok(self, sem):
        return (sem, self.counts[sem])


class Builder:
    ENG_SEM = {"tensor": "pe", "vector": "ve", "scalar": "se", "gpsimd": "ge",
               "sync": "sy"}
    DMA_SEM = {"sync": "dq", "gpsimd": "dg", "scalar": "ds"}

    def __init__(self, nc, asm, t):
        self.nc = nc
        self.asm = asm
        self.t = t

    def st(self, eng, fn, waits=(), drain=False, tag=""):
        sem = self.ENG_SEM[eng]
        self.asm.step(eng, fn, waits=waits, incs=[(sem, 1)], drain=drain, tag=tag)
        return self.asm.tok(sem)

    def dma(self, queue, out_ap, in_ap, waits=(), tag="", slow=False):
        sem = self.DMA_SEM[queue]

        def fn(eng, out_ap=out_ap, in_ap=in_ap, slow=slow):
            return eng.dma_start(out=out_ap, in_=in_ap,
                                 allow_slow_non_contiguous=slow)

        self.asm.step(queue, fn, waits=waits, incs=[(sem, 16)], tag=tag)
        return self.asm.tok(sem)

    def fence(self, engs=("pe", "ve", "se")):
        return [self.asm.tok(e) for e in engs]


class WCache:

    def __init__(self):
        self.seen = set()
        self.fb_tok = [None, None]
        self.par = 0
        self.buf_save = {}

    def stream(self, bld, dst, src32, scr16, key, waits=()):
        bkey = (dst.tensor.name, dst.offset)
        w_extra = [self.buf_save.pop(bkey)] if bkey in self.buf_save else []
        if key in self.seen:
            return bld.dma("sync", _match(dst, scr16), scr16,
                           waits=list(waits) + w_extra, tag=f"w${key}")
        self.seen.add(key)
        t = bld.t
        n = dst.shape[-1] if len(dst.shape) == 2 else None
        assert n is not None
        fw = 1024
        tok_c = None
        tok_s = None
        for i in range((n + fw - 1) // fw):
            lo = i * fw
            hi = min(n, lo + fw)
            p = self.par % 2
            self.par += 1
            fb = t.fp_conv[p][:, 0:hi - lo]
            srcp = _ccols(src32, lo, hi)
            w = list(waits) + w_extra + ([self.fb_tok[p]] if self.fb_tok[p] else [])
            w_extra = []
            tok_l = bld.dma("sync", _match(fb, srcp), srcp, waits=w, tag=f"wl${key}")

            def cv(eng, fb=fb, dst=dst, lo=lo, hi=hi):
                return eng.copy(out=dst[:, lo:hi], in_=fb)

            tok_c = bld.st("scalar", cv, waits=[tok_l], tag=f"wc${key}")
            self.fb_tok[p] = tok_c
            scrp = _ccols(scr16, lo, hi)
            tok_s = bld.dma("gpsimd", scrp, _match(dst[:, lo:hi], scrp),
                            waits=[tok_c], tag=f"ws${key}")
        self.buf_save[bkey] = tok_s
        return tok_c


def _rope_tables(theta, scale):
    i = np.arange(64, dtype=np.float64)
    inv = theta ** (-i / 64.0)
    pos = np.arange(T, dtype=np.float64)
    ang = pos[:, None] * inv[None, :]
    return (np.cos(ang) * scale).astype(np.float32), \
           (np.sin(ang) * scale).astype(np.float32)


def _consts():
    c = {}
    c["ident"] = np.eye(128, dtype=ml_dtypes.bfloat16)
    m = np.zeros((128, 128), dtype=np.float32)
    m[np.triu_indices(128, 1)] = NEG_BIG
    c["cmask"] = m
    sc = 1.0 / np.sqrt(128.0)
    cq, sq = _rope_tables(1e4, sc)
    ck, sk = _rope_tables(1e4, 1.0)
    c["ropeA"] = np.concatenate([cq, sq, ck, sk], axis=1)
    cq, sq = _rope_tables(1e6, sc)
    ck, sk = _rope_tables(1e6, 1.0)
    c["ropeB"] = np.concatenate([cq, sq, ck, sk], axis=1)
    return c


def _r16(ap):
    return ap.rearrange("(a p) m -> p a m", p=128)


def _rslab(ap, mt, msz=128):
    r = ap.rearrange("(a p) m -> p a m", p=128)
    return r[:, :, mt * msz:(mt + 1) * msz]


def _ccols(ap, lo, hi):
    if len(ap.shape) == 2:
        return ap[:, lo:hi]
    m = ap.shape[2]
    if ap.shape[1] == 1:
        return ap[:, :, lo:hi]
    assert lo % m == 0 and hi % m == 0, (lo, hi, m)
    return ap[:, lo // m:hi // m, :]


def _match(sb_flat, like):
    if len(like.shape) == 2:
        return sb_flat
    return sb_flat.rearrange("p (a m) -> p a m", m=like.shape[2])


def build_program():
    nc = bass.Bass("TRN2", target_bir_lowering=False)
    t = type("NS", (), {})()

    t.x = nc.dram_tensor("x", [BC, T, D], F32, kind="ExternalInput").ap()
    w_in = {}
    for name, shape in [
        ("Wq_A", [D, D]), ("Wk_A", [D, HD]), ("Wv_A", [D, HD]), ("Wo_A", [D, D]),
        ("wq_B", [D, D]), ("wk_B", [D, D]), ("wv_B", [D, D]), ("wo_B", [D, D]),
        ("w_router", [D, N_EXP]),
        ("w_gate_e", [N_EXP, D, MOE_I]), ("w_up_e", [N_EXP, D, MOE_I]),
        ("w_down_e", [N_EXP, MOE_I, D]),
        ("w_sh_gate", [D, SH_I]), ("w_sh_up", [D, SH_I]), ("w_sh_down", [SH_I, D]),
        ("w_sh_eg", [D, 1]),
    ]:
        w_in[name] = nc.dram_tensor(name, shape, F32, kind="ExternalInput").ap()
    t.w_in = w_in
    for nm, sh in [("bq_Ar", [1, D]), ("bk_Ar", [1, HD]), ("bv_Acol", [128, 16]),
                   ("bo_A", [1, D]), ("bq_Br", [1, D]), ("bk_Br", [1, D]),
                   ("bv_Bcol", [128, 16]), ("w_ln1", [D]), ("w_ln2", [D]),
                   ("cmask", [128, 128]), ("ropeA", [T, 256]), ("ropeB", [T, 256])]:
        setattr(t, nm + "_in", nc.dram_tensor(nm, sh, F32, kind="ExternalInput").ap())
    t.ident_in = nc.dram_tensor("ident", [128, 128], BF16, kind="ExternalInput").ap()
    t.out = nc.dram_tensor("out", [BC, T, D], F32, kind="ExternalOutput").ap()

    def scr(name, shape, dtype=BF16):
        return nc.dram_tensor(name, shape, dtype).ap()

    t.wb = {}
    for name, shape in [
        ("Wq_A", [D, D]), ("Wk_A", [D, HD]), ("Wv_A", [D, HD]), ("Wo_A", [D, D]),
        ("wq_B", [D, D]), ("wk_B", [D, D]), ("wv_B", [D, D]), ("wo_B", [D, D]),
        ("wre", [D, 9]),
        ("w_gate_e", [N_EXP, D, MOE_I]), ("w_up_e", [N_EXP, D, MOE_I]),
        ("w_down_e", [N_EXP, MOE_I, D]),
        ("w_sh_gate", [D, SH_I]), ("w_sh_up", [D, SH_I]), ("w_sh_down", [SH_I, D]),
    ]:
        t.wb[name] = scr("bf_" + name, shape)
    t.x2_scr = scr("x2_scr", [BC, T, D], F32)
    t.xin_scr = scr("xin_scr", [T, D], F32)
    t.qA_scr = scr("qA_scr", [NH, 128, NT, HD])
    t.qB_scr = scr("qB_scr", [NH, 128, NT, HD])
    t.kB_scr = scr("kB_scr", [NH, 128, NT, HD])
    t.vB_scr = scr("vB_scr", [NH, 128, NT, HD])
    t.boe_scr = scr("boe_scr", [2, D], F32)

    import contextlib
    stack = contextlib.ExitStack()
    sb = lambda n, c, d=BF16: stack.enter_context(nc.sbuf_tensor(n, [128, c], d))
    t.R1 = sb("R1", 32768)
    t.R2 = sb("R2", 16384)
    t.SP = sb("SP", 21504)
    t.FP = sb("FP", 14848, F32)
    t.CB = sb("CB", 512, F32)
    t.KB = sb("KB", 4496)
    t.ps_a = stack.enter_context(nc.psum_tensor("ps_a", [128, 2048], F32))
    t.ps_b = stack.enter_context(nc.psum_tensor("ps_b", [128, 2048], F32))

    FP = t.FP
    t.fp_tab = FP[:, 0:4096]
    t.fp_qtmp = (FP[:, 4096:4608], FP[:, 4608:5120])
    t.fp_rs = FP[:, 5120:5376]
    t.fp_bias = FP[:, 6144:8192]
    t.fp_borow = FP[0:1, 6144:8192]
    t.fp_acc = FP[:, 0:8192]
    t.fp_conv = (FP[:, 8192:9216], FP[:, 9216:10240])
    t.fp_sig = FP[:, 10240:10368]
    t.fp_st = FP[:, 10368:10560]
    t.c_all = FP[:, 10560:10704]
    t.fp_m2 = FP[:, 10704:10720]
    t.fp_r2 = FP[:, 10720:10736]
    t.fp_rr = FP[:, 10736:10752]
    t.fp_s1 = FP[:, 5376:5392]
    t.fp_t1 = FP[:, 10752:12800]
    t.fp_t2 = FP[:, 12800:14848]
    t.fp_t3 = FP[:, 0:2048]

    t.cb_cmask = t.CB[:, 0:128]
    t.cb_bvA = t.CB[:, 128:144]
    t.cb_bvB = t.CB[:, 144:160]

    t.kb_ident = t.KB[:, 0:128]
    t.kb_diag = (t.KB[:, 128:256], t.KB[:, 256:384])
    t.kb_boe = t.KB[:, 384:2432]
    t.kb_ln = t.KB[:, 2432:4480]
    t.kb_bvtmp = t.KB[:, 4480:4496]

    t.const_vals = [EPS_LN, EPS_RMS, EPS_LN * EPS_RMS, 1.0 + EPS_RMS]
    for i, v in enumerate(t.const_vals):
        nc.const_aps.register(F32, v, t.CB[:, 200 + i:201 + i])

    sems = {}
    for s in ["pe", "ve", "se", "ge", "sy", "dq", "dg", "ds"]:
        sems[s] = stack.enter_context(nc.semaphore(s))
    t.sems = sems

    asm = Asm()
    for s in sems:
        asm.add_sem(s)
    bld = Builder(nc, asm, t)
    emit_all(bld)
    global _LAST_ASM
    _LAST_ASM = asm

    with stack:
        with nc.Block() as block:
            def make(engname):
                def closure(eng):
                    if engname != "gpsimd":
                        eng.wait_ge(sems["ge"], 1)
                    for (e, fn, waits, incs, drain, tag) in asm.steps:
                        if e != engname:
                            continue
                        for (s, v) in waits:
                            if v > 0:
                                eng.wait_ge(sems[s], v)
                        if drain:
                            eng.drain()
                        ins = fn(eng)
                        for (s, n) in incs:
                            ins.then_inc(sems[s], n)
                return closure
            block.sync(make("sync"))
            block.scalar(make("scalar"))
            block.vector(make("vector"))
            block.tensor(make("tensor"))
            block.gpsimd(make("gpsimd"))
    return nc


def emit_all(bld):
    t = bld.t
    a = bld.asm
    t.wc = WCache()

    def cmemset(eng):
        ins = None
        for i, v in enumerate(t.const_vals):
            ins = eng.memset(t.CB[:, 200 + i:201 + i], v)
        return ins
    t.tok_cm = bld.st("gpsimd", cmemset, tag="c:memset")
    toks = []
    toks.append(bld.dma("gpsimd", t.kb_ident, t.ident_in, tag="c:ident"))
    toks.append(bld.dma("gpsimd", t.cb_cmask, t.cmask_in, tag="c:cmask"))
    toks.append(bld.dma("gpsimd", t.cb_bvA, t.bv_Acol_in, tag="c:bvA"))
    toks.append(bld.dma("gpsimd", t.cb_bvB, t.bv_Bcol_in, tag="c:bvB"))
    t.bcast = lambda ap, n: bass.AP(ap.tensor, ap.offset, [[0, 128], [1, n]])
    t.tok_consts = toks
    t.tok_ident = toks[0]

    for b in range(BC):
        t.b = b
        if "A1" in _BUILD_PHASES:
            phase_h_build(bld, b, src="x")
        if "A2" in _BUILD_PHASES:
            phase_QKV(bld, b, "A")
        if "A3" in _BUILD_PHASES:
            phase_attn(bld, b, "A")
        if "A4" in _BUILD_PHASES:
            phase_wo(bld, b, "A")
        if "B2" in _BUILD_PHASES:
            phase_hB(bld, b)
            phase_QKV(bld, b, "B")
        if "B3" in _BUILD_PHASES:
            phase_attn(bld, b, "B")
        if "B4" in _BUILD_PHASES:
            phase_wo(bld, b, "B")
        if "B5" in _BUILD_PHASES:
            phase_router(bld, b)
        if "B6" in _BUILD_PHASES:
            phase_moe(bld, b)

    def fin(eng):
        return eng.nop()
    a.step("gpsimd", fin, waits=[a.tok("dg"), a.tok("dq"), a.tok("ds")],
           incs=[("ge", 1)], tag="final")


def load_bc_bf16(bld, dst_bf16, src_row_ap, stage_fp, waits, tag=""):
    t = bld.t
    tok_l = bld.dma("gpsimd", stage_fp, t.bcast(src_row_ap, D), waits=waits,
                    tag="bc:" + tag)

    def cv(eng):
        return eng.copy(out=dst_bf16, in_=stage_fp)
    return bld.st("scalar", cv, waits=[tok_l], tag="bccv:" + tag)


def ln_stats(bld, src, n, waits, tag=""):
    t = bld.t
    stats6 = t.fp_st[:, 0:24]
    mv = t.fp_st[:, 24:26]

    def bn(eng, src=src, n=n):
        nblk = n // 512
        for i in range(nblk):
            eng.bn_stats(out=stats6[:, 6 * i:6 * i + 6], in_=src[:, 512 * i:512 * (i + 1)])
        eng.drain()
        return eng.bn_aggr(out=mv, in_=stats6[:, 0:6 * nblk])
    return bld.st("vector", bn, waits=waits, drain=True, tag="bn" + tag)


def ln_scale(bld, tok_bn, eps, tag=""):
    t = bld.t
    mv = t.fp_st[:, 24:26]
    rstd = t.fp_st[:, 26:27]
    nmr = t.fp_st[:, 27:28]

    def sq(eng):
        return eng.activation(out=rstd, in_=mv[:, 1:2], func=AF.Sqrt, bias=eps)
    tok_sq = bld.st("scalar", sq, waits=[tok_bn], tag="sq" + tag)

    def rcp(eng):
        eng.reciprocal(out=rstd, in_=rstd)
        eng.drain()
        return eng.tensor_scalar(out=nmr, in0=mv[:, 0:1], scalar1=rstd,
                                 scalar2=-1.0, op0=ALU.mult, op1=ALU.mult)
    return bld.st("vector", rcp, waits=[tok_sq], drain=True, tag="rcp" + tag), rstd, nmr


def phase_h_build(bld, b, src):
    t = bld.t
    a = bld.asm
    htm = (t.SP[:, 0:2048], t.SP[:, 2048:4096])
    xin = (t.fp_t1, t.fp_t2)
    guard = bld.fence(("pe", "ve", "se"))
    tok_x_rd = [None, None]
    tok_h_rd = [None, None]
    tok_ps = [None, None]

    for tt in range(NT):
        xb = xin[tt % 2]
        w = guard + ([tok_x_rd[tt % 2]] if tok_x_rd[tt % 2] else [])
        tok_x = bld.dma("gpsimd", xb, t.x[b, tt * 128:(tt + 1) * 128, :], waits=w,
                        tag=f"A1:x{tt}")
        tok_bn = ln_stats(bld, xb, 2048, [tok_x], tag=f"A1.{tt}")
        tok_sc, rstd, nmr = ln_scale(bld, tok_bn, EPS_LN, tag=f"A1.{tt}")
        hb = htm[tt % 2]
        w = [tok_sc] + ([tok_h_rd[tt % 2]] if tok_h_rd[tt % 2] else []) + guard[:1]

        def app(eng, xb=xb, hb=hb, rstd=rstd, nmr=nmr):
            return eng.activation(out=hb, in_=xb, func=AF.Identity, bias=nmr, scale=rstd)
        tok_app = bld.st("scalar", app, waits=w, tag=f"A1:app{tt}")
        tok_x_rd[tt % 2] = tok_app

        ps = (t.ps_a, t.ps_b)[tt % 2]
        w = [tok_app, t.tok_ident] + ([tok_ps[tt % 2]] if tok_ps[tt % 2] else [])

        def tr(eng, hb=hb, ps=ps):
            ins = None
            for k in range(16):
                ins = eng.matmul(ps[:, k * 128:(k + 1) * 128],
                                 hb[:, k * 128:(k + 1) * 128],
                                 t.kb_ident, start=True, stop=True)
            return ins
        tok_tr = bld.st("tensor", tr, waits=w, tag=f"A1:tr{tt}")
        tok_h_rd[tt % 2] = tok_tr

        dst = t.R1[:].rearrange("p (a m) -> p a m", a=16)[:, :, tt * 128:(tt + 1) * 128]

        def ev(eng, ps=ps, dst=dst):
            return eng.tensor_copy(out=dst, in_=ps[:].rearrange("p (a m) -> p a m", a=16))
        tok_ev = bld.st("vector", ev, waits=[tok_tr], drain=True, tag=f"A1:ev{tt}")
        tok_ps[tt % 2] = tok_ev
    t.tok_hT = a.tok("ve")


def phase_hB(bld, b):
    t = bld.t
    a = bld.asm
    guard = bld.fence(("pe", "ve", "se")) + [a.tok("dg")]
    tok_ln = load_bc_bf16(bld, t.kb_ln, t.w_ln1_in, t.fp_t1, guard, tag="ln1")
    htm = (t.SP[:, 0:2048], t.SP[:, 2048:4096])
    xb2 = (t.fp_t1, t.fp_t2)
    tok_xb = [None, None]
    tok_htm = [None, None]
    tok_ps = [None, None]
    for tt in range(NT):
        xb = xb2[tt % 2]
        w = guard + [tok_ln] + ([tok_xb[tt % 2]] if tok_xb[tt % 2] else [])
        tok_x = bld.dma("gpsimd", xb, t.x2_scr[b, tt * 128:(tt + 1) * 128, :],
                        waits=w, tag=f"hB:x{tt}")
        m2s = t.fp_m2[:, tt:tt + 1]
        s1s = t.fp_s1[:, tt:tt + 1]
        nb = t.fp_st[:, 30:31]

        def negb(eng, m2s=m2s, s1s=s1s, nb=nb):
            return eng.tensor_scalar(out=nb, in0=m2s, scalar1=s1s, scalar2=-1.0,
                                     op0=ALU.mult, op1=ALU.mult)
        tok_nb = bld.st("vector", negb, waits=[tok_x], drain=True)

        def ap1(eng, xb=xb, s1s=s1s, nb=nb):
            return eng.activation(out=xb, in_=xb, func=AF.Identity, bias=nb, scale=s1s)
        tok_a1 = bld.st("scalar", ap1, waits=[tok_nb, tok_ln], tag=f"hB:a{tt}")
        hb = htm[tt % 2]
        w = [tok_a1] + guard[:1] + ([tok_htm[tt % 2]] if tok_htm[tt % 2] else [])

        def ap2(eng, xb=xb, hb=hb):
            return eng.tensor_mul(out=hb, in0=xb, in1=t.kb_ln)
        tok_h = bld.st("vector", ap2, waits=w, drain=True, tag=f"hB:m{tt}")
        tok_xb[tt % 2] = tok_h
        ps = (t.ps_a, t.ps_b)[tt % 2]
        w = [tok_h] + ([tok_ps[tt % 2]] if tok_ps[tt % 2] else []) + guard[:1]

        def tr(eng, hb=hb, ps=ps):
            ins = None
            for k in range(16):
                ins = eng.matmul(ps[:, k * 128:(k + 1) * 128],
                                 hb[:, k * 128:(k + 1) * 128], t.kb_ident,
                                 start=True, stop=True)
            return ins
        tok_tr = bld.st("tensor", tr, waits=w, tag=f"hB:tr{tt}")
        tok_htm[tt % 2] = tok_tr
        dst = t.R1[:].rearrange("p (a m) -> p a m", a=16)[:, :, tt * 128:(tt + 1) * 128]

        def ev(eng, ps=ps, dst=dst):
            return eng.tensor_copy(out=dst, in_=ps[:].rearrange("p (a m) -> p a m", a=16))
        tok_ev = bld.st("vector", ev, waits=[tok_tr], drain=True, tag=f"hB:ev{tt}")
        tok_ps[tt % 2] = tok_ev
    t.tok_hT = a.tok("ve")


def phase_QKV(bld, b, stage):
    t = bld.t
    a = bld.asm
    wc = t.wc
    guard = bld.fence(("pe", "ve", "se")) + [a.tok("dg"), a.tok("dq")]

    rope_in = t.ropeA_in if stage == "A" else t.ropeB_in
    tok_tab = bld.dma("gpsimd", t.fp_tab.rearrange("p (a m) -> p a m", m=256),
                      rope_in.rearrange("(a p) m -> p a m", p=128),
                      waits=guard, tag=f"{stage}:tab")
    tabv = t.fp_tab.rearrange("p (a m) -> p a m", m=256)

    if stage == "A":
        specs = [("q", "Wq_A", t.bq_Ar_in, 4, True, "hbm", t.qA_scr),
                 ("k", "Wk_A", t.bk_Ar_in, 1, True, "sb_k", None),
                 ("v", "Wv_A", None, 1, False, "sb_v", None)]
    else:
        specs = [("q", "wq_B", t.bq_Br_in, 4, True, "hbm", t.qB_scr),
                 ("k", "wk_B", t.bk_Br_in, 4, True, "hbm", t.kB_scr),
                 ("v", "wv_B", None, 4, False, "hbm", t.vB_scr)]

    wslab = (t.SP[:, 0:8192], t.SP[:, 8192:16384])
    t.kT_A = t.SP[:, 16384:18432]
    t.vtm_A = t.SP[:, 18432:20480]
    ostg = (t.SP[:, 20480:20992], t.SP[:, 20992:21504])

    slots = [(t.ps_a, i) for i in range(4)] + [(t.ps_b, i) for i in range(4)]
    tok_slot = [None] * 8
    si = [0]
    tok_wslab_rd = [None, None]
    tok_ostg_rd = [None, None]
    tok_ka = [None, None]

    for (nm, wname, bias_in, nch, do_rope, route, out_scr) in specs:
        cw = 512 if nch == 4 else 128
        if bias_in is not None:
            bw = bias_in.shape[-1]
            tok_bb = bld.dma("gpsimd", t.fp_bias[:, 0:bw],
                             bass.AP(bias_in.tensor, bias_in.offset, [[0, 128], [1, bw]]),
                             waits=guard + [a.tok("ve")], tag=f"{stage}{nm}:bb")
        else:
            tok_bb = None
        for ch in range(nch):
            par = ch % 2
            w = guard + ([tok_wslab_rd[par]] if tok_wslab_rd[par] else [])
            src32 = _rslab(t.w_in[wname], ch, cw)
            scr16 = _rslab(t.wb[wname], ch, cw)
            tok_w = wc.stream(bld, wslab[par][:, 0:16 * cw], src32, scr16,
                              f"{wname}:{ch}", waits=w)
            for tt in range(NT):
                sl = si[0] % 6
                si[0] += 1
                pst, pso = slots[sl]
                psap = pst[:, pso * 512:pso * 512 + cw]
                w = [tok_w, t.tok_hT] + ([tok_slot[sl]] if tok_slot[sl] else [])

                def mm(eng, wsl=wslab[par], psap=psap, tt=tt, cw=cw):
                    ins = None
                    for kt in range(16):
                        ins = eng.matmul(
                            psap,
                            t.R1[:, kt * 2048 + tt * 128:kt * 2048 + (tt + 1) * 128],
                            wsl[:, kt * cw:(kt + 1) * cw],
                            start=(kt == 0), stop=(kt == 15))
                    return ins
                tok_mm = bld.st("tensor", mm, waits=w, tag=f"{stage}{nm}:mm{ch}.{tt}")
                tok_wslab_rd[par] = tok_mm

                opar = si[0] % 2
                ob = ostg[opar][:, 0:cw]
                if do_rope:
                    qb = t.fp_qtmp[opar][:, 0:cw]
                    w = [tok_mm, tok_bb]

                    def evb(eng, psap=psap, qb=qb, cw=cw, ch=ch):
                        return eng.tensor_add(out=qb, in0=psap,
                                              in1=t.fp_bias[:, ch * cw:(ch + 1) * cw])
                    tok_ev = bld.st("vector", evb, waits=w, drain=True,
                                    tag=f"{stage}{nm}:ev{ch}.{tt}")
                    tok_slot[sl] = tok_ev
                    w2 = [tok_tab, tok_ev] + ([tok_ostg_rd[opar]] if tok_ostg_rd[opar] else [])
                    toff = 0 if nm == "q" else 128

                    def rope(eng, qb=qb, ob=ob, tt=tt, cw=cw, toff=toff):
                        rs1 = t.fp_rs[:, 0:64]
                        rs2 = t.fp_rs[:, 64:128]
                        cs = tabv[:, tt, toff + 0:toff + 64]
                        sn = tabv[:, tt, toff + 64:toff + 128]
                        ins = None
                        for j in range(cw // 128):
                            x1 = qb[:, j * 128:j * 128 + 64]
                            x2 = qb[:, j * 128 + 64:(j + 1) * 128]
                            eng.tensor_mul(out=rs1, in0=x1, in1=cs)
                            eng.tensor_mul(out=rs2, in0=x2, in1=sn)
                            eng.drain()
                            eng.tensor_sub(out=ob[:, j * 128:j * 128 + 64], in0=rs1, in1=rs2)
                            eng.tensor_mul(out=rs1, in0=x2, in1=cs)
                            eng.tensor_mul(out=rs2, in0=x1, in1=sn)
                            eng.drain()
                            ins = eng.tensor_add(out=ob[:, j * 128 + 64:(j + 1) * 128],
                                                 in0=rs1, in1=rs2)
                        return ins
                    tok_o = bld.st("vector", rope, waits=w2, drain=True,
                                   tag=f"{stage}{nm}:rp{ch}.{tt}")
                elif route == "hbm":
                    w = [tok_mm] + ([tok_ostg_rd[opar]] if tok_ostg_rd[opar] else [])

                    def evn(eng, psap=psap, ob=ob):
                        return eng.copy(out=ob, in_=psap)
                    tok_o = bld.st("scalar", evn, waits=w, tag=f"{stage}{nm}:ev{ch}.{tt}")
                    tok_slot[sl] = tok_o
                else:
                    def vev(eng, psap=psap, dst=t.vtm_A[:, tt * 128:(tt + 1) * 128]):
                        return eng.copy(out=dst, in_=psap)
                    tok_slot[sl] = bld.st("scalar", vev, waits=[tok_mm], tag=f"Av:ev{tt}")
                    continue

                if route == "hbm":
                    nheads = cw // 128
                    dst = out_scr[ch * nheads:(ch + 1) * nheads, :, tt:tt + 1, :]
                    dst = dst.rearrange("h p t d -> p h (t d)")
                    tok_d = bld.dma("gpsimd", dst,
                                    ob.rearrange("p (h d) -> p h d", d=128),
                                    waits=[tok_o], tag=f"{stage}{nm}:o{ch}.{tt}")
                    tok_ostg_rd[opar] = tok_d
                else:
                    ksl = 6 + (tt % 2)
                    pst2, pso2 = slots[ksl]
                    ps2 = pst2[:, pso2 * 512:pso2 * 512 + 128]
                    w = [tok_o] + ([tok_ka[tt % 2]] if tok_ka[tt % 2] else [])

                    def ktr(eng, ob=ob, ps2=ps2):
                        return eng.matmul(ps2, ob[:, 0:128], t.kb_ident,
                                          start=True, stop=True)
                    tok_tr = bld.st("tensor", ktr, waits=w, tag=f"Ak:tr{tt}")
                    tok_ostg_rd[opar] = tok_tr

                    def kev(eng, ps2=ps2, dst=t.kT_A[:, tt * 128:(tt + 1) * 128]):
                        return eng.tensor_copy(out=dst, in_=ps2)
                    tok_ka[tt % 2] = bld.st("vector", kev, waits=[tok_tr], drain=True)
    t.tok_qkv = (a.tok("ve"), a.tok("se"), a.tok("pe"), a.tok("dg"))


def phase_attn(bld, b, stage):
    t = bld.t
    a = bld.asm
    causal = (stage == "B")
    q_scr = t.qA_scr if stage == "A" else t.qB_scr

    ptm = (t.SP[:, 0:2048], t.SP[:, 2048:4096])
    vh_pair = (t.SP[:, 4096:6144], t.SP[:, 6144:8192])
    qtm = t.SP[:, 8192:10240]
    ktm = t.SP[:, 10240:12288]
    qfm_pair = (t.SP[:, 12288:14336], t.SP[:, 14336:16384])
    kfm_pair = (t.SP[:, 16384:18432], t.SP[:, 18432:20480])
    guard = bld.fence(("pe", "ve", "se")) + [bld.asm.tok("dg")]

    tok_qtm_rd = None
    tok_ktm_rd = None
    tok_qfm_rd = [None, None]
    tok_kfm_rd = [None, None]
    tok_v_rd = [None, None]
    tok_ptm_rd = [None, None]
    tok_sig_rd = [None] * 4
    tok_diag_rd = [None, None]
    tok_psa_rd = None
    tok_trb_ev = None
    tok_pv_ev = None
    last_pv = None

    for h in range(NH):
        par = h % 2
        w = guard + ([tok_qtm_rd] if tok_qtm_rd else [])
        tok_q = bld.dma("gpsimd", qtm, q_scr[h].rearrange("p a m -> p (a m)"),
                        waits=w, tag=f"{stage}at:q{h}")
        wps = [x for x in (tok_trb_ev, tok_pv_ev) if x]

        def qtr(eng, tok=None):
            ins = None
            for k in range(16):
                ins = eng.matmul(t.ps_b[:, k * 128:(k + 1) * 128],
                                 qtm[:, k * 128:(k + 1) * 128], t.kb_ident,
                                 start=True, stop=True)
            return ins
        tok_qt = bld.st("tensor", qtr, waits=[tok_q] + wps, tag=f"{stage}at:qtr{h}")
        tok_qtm_rd = tok_qt
        qh_ = qfm_pair[par]
        w = [tok_qt] + ([tok_qfm_rd[par]] if tok_qfm_rd[par] else [])

        def qev(eng, qh_=qh_):
            return eng.copy(out=qh_, in_=t.ps_b[:])
        tok_qe = bld.st("scalar", qev, waits=w, tag=f"{stage}at:qev{h}")

        if causal:
            w = guard + ([tok_ktm_rd] if tok_ktm_rd else [])
            tok_kd = bld.dma("gpsimd", ktm, t.kB_scr[h].rearrange("p a m -> p (a m)"),
                             waits=w, tag=f"at:k{h}")

            def ktr2(eng):
                ins = None
                for k in range(16):
                    ins = eng.matmul(t.ps_b[:, k * 128:(k + 1) * 128],
                                     ktm[:, k * 128:(k + 1) * 128], t.kb_ident,
                                     start=True, stop=True)
                return ins
            tok_kt2 = bld.st("tensor", ktr2, waits=[tok_kd, tok_qe], tag=f"at:ktr{h}")
            tok_ktm_rd = tok_kt2
            kT = kfm_pair[par]
            w = [tok_kt2] + ([tok_kfm_rd[par]] if tok_kfm_rd[par] else [])

            def kev2(eng, kT=kT):
                return eng.copy(out=kT, in_=t.ps_b[:])
            tok_ke = bld.st("scalar", kev2, waits=w, tag=f"at:kev{h}")
            tok_k = tok_ke
            w = guard + ([tok_v_rd[par]] if tok_v_rd[par] else [])
            tok_v = bld.dma("gpsimd", vh_pair[par],
                            t.vB_scr[h].rearrange("p a m -> p (a m)"), waits=w,
                            tag=f"at:v{h}")
            vtm = vh_pair[par]
        else:
            tok_k = t.tok_qkv[0]
            tok_v = (t.tok_qkv[1],)[0]
            kT = t.kT_A
            vtm = t.vtm_A
        tok_q = tok_qe

        tok_s = None
        for half in range(2):
            tok_half_tr = None
            for qi in range(8):
                qt = half * 8 + qi
                nk = (qt + 1) * 128 if causal else 2048
                w = [tok_q, tok_k] + ([tok_psa_rd] if tok_psa_rd else [])

                def smm(eng, qh_=qfm_pair[par], kT=kT, qt=qt, nk=nk):
                    ins = None
                    lhs = qh_[:, qt * 128:(qt + 1) * 128]
                    for c0 in range(0, nk, 512):
                        c1 = min(nk, c0 + 512)
                        ins = eng.matmul(t.ps_a[:, c0:c1], lhs, kT[:, c0:c1],
                                         start=True, stop=True)
                    return ins
                tok_s = bld.st("tensor", smm, waits=w, tag=f"{stage}at:S{h}.{qt}")
                tok_last = tok_s
                if causal:
                    def mka(eng, qt=qt):
                        return eng.tensor_add(
                            out=t.ps_a[:, qt * 128:(qt + 1) * 128],
                            in0=t.ps_a[:, qt * 128:(qt + 1) * 128], in1=t.cb_cmask)
                    tok_last = bld.st("vector", mka, waits=[tok_s, t.tok_consts[1]],
                                      drain=True, tag=f"at:mk{qt}")
                sl = qt % 4
                nmx = t.fp_sig[:, sl * 4:sl * 4 + 1]
                sg = t.fp_sig[:, sl * 4 + 1:sl * 4 + 2]
                rc = t.fp_sig[:, sl * 4 + 2:sl * 4 + 3]
                w = [tok_last] + ([tok_sig_rd[sl]] if tok_sig_rd[sl] else [])

                def mx(eng, nmx=nmx, nk=nk):
                    return eng.reduce_max(out=nmx, in_=t.ps_a[:, 0:nk], axis=AX,
                                          negate=True)
                tok_mx = bld.st("vector", mx, waits=w, drain=True, tag=f"at:mx{qt}")
                pb = ptm[qt % 2]
                w = [tok_mx] + ([tok_ptm_rd[qt % 2]] if tok_ptm_rd[qt % 2] else [])

                def ex(eng, pb=pb, nmx=nmx, sg=sg, nk=nk):
                    return eng.activation(out=pb[:, 0:nk], in_=t.ps_a[:, 0:nk],
                                          func=AF.Exp, bias=nmx, scale=1.0,
                                          accum_out=sg)
                tok_ex = bld.st("scalar", ex, waits=w, tag=f"at:ex{qt}")
                tok_psa_rd = tok_ex
                tok_sig_rd[sl] = tok_ex
                dgb = t.kb_diag[qt % 2]
                w = [tok_ex] + ([tok_diag_rd[qt % 2]] if tok_diag_rd[qt % 2] else [])

                def rcd(eng, sg=sg, rc=rc, dgb=dgb):
                    eng.reciprocal(out=rc, in_=sg)
                    eng.drain()
                    return eng.tensor_scalar(out=dgb, in0=t.kb_ident, scalar1=rc,
                                             scalar2=None, op0=ALU.mult)
                tok_rcd = bld.st("vector", rcd, waits=w, drain=True, tag=f"at:rc{qt}")
                tok_sig_rd[sl] = tok_rcd

                nkt = (qt + 1) if causal else 16
                tok_t = None
                for g0 in range(0, nkt, 8):
                    g1 = min(nkt, g0 + 8)
                    w = [tok_rcd] + ([tok_trb_ev] if tok_trb_ev else [])

                    def trm(eng, pb=pb, dgb=dgb, g0=g0, g1=g1):
                        ins = None
                        for k in range(g0, g1):
                            ins = eng.matmul(
                                t.ps_b[:, (k - g0) * 128:(k - g0 + 1) * 128],
                                pb[:, k * 128:(k + 1) * 128], dgb,
                                start=True, stop=True)
                        return ins
                    tok_t = bld.st("tensor", trm, waits=w, tag=f"at:tr{qt}.{g0}")
                    dview = t.R1[:, 0:16384].rearrange("p (a m) -> p a m", a=16)[
                        :, g0:g1, qi * 128:(qi + 1) * 128]

                    def evt(eng, dview=dview, g0=g0, g1=g1):
                        return eng.copy(
                            out=dview,
                            in_=t.ps_b[:, 0:(g1 - g0) * 128].rearrange(
                                "p (a m) -> p a m", a=(g1 - g0)))
                    tok_trb_ev = bld.st("scalar", evt, waits=[tok_t], tag=f"at:evt{qt}.{g0}")
                tok_ptm_rd[qt % 2] = tok_t
                tok_diag_rd[qt % 2] = tok_t
                tok_half_tr = tok_trb_ev

            nkt_h = 8 * (half + 1) if causal else 16
            w = [tok_half_tr, tok_v] + ([tok_pv_ev] if tok_pv_ev else [])

            def pv(eng, vtm=vtm, half=half, nkt_h=nkt_h, causal=causal):
                ins = None
                for kt in range(nkt_h):
                    q0 = kt * 128 - half * 1024 if causal else 0
                    for c0 in (0, 512):
                        c0a = max(c0, q0)
                        c1 = c0 + 512
                        if c0a >= c1:
                            continue
                        if causal:
                            lk = min(nkt_h - 1, (half * 1024 + c1 - 1) // 128)
                        else:
                            lk = nkt_h - 1
                        ins = eng.matmul(t.ps_b[:, 1024 + c0a:1024 + c1],
                                         vtm[:, kt * 128:(kt + 1) * 128],
                                         t.R1[:, kt * 1024 + c0a: kt * 1024 + c1],
                                         start=(kt == 0), stop=(kt == lk))
                return ins
            tok_pv = bld.st("tensor", pv, waits=w, tag=f"{stage}at:pv{h}.{half}")
            last_pv = tok_pv
            if h < 8:
                cdst = t.R2[:, h * 2048 + half * 1024: h * 2048 + (half + 1) * 1024]
            else:
                cdst = t.R1[:, 16384 + (h - 8) * 2048 + half * 1024:
                            16384 + (h - 8) * 2048 + (half + 1) * 1024]

            def evc(eng, cdst=cdst):
                return eng.tensor_copy(out=cdst, in_=t.ps_b[:, 1024:2048])
            tok_pv_ev = bld.st("vector", evc, waits=[tok_pv], drain=True,
                               tag=f"{stage}at:evc{h}.{half}")
        tok_qfm_rd[par] = tok_s
        if causal:
            tok_kfm_rd[par] = tok_s
            tok_v_rd[par] = last_pv
    t.tok_attn = (a.tok("ve"), a.tok("se"), a.tok("pe"))


def phase_wo(bld, b, stage):
    t = bld.t
    a = bld.asm
    wc = t.wc
    wname = "Wo_A" if stage == "A" else "wo_B"
    guard = [t.tok_attn[2], t.tok_attn[0], t.tok_attn[1]]

    slab_toks = []
    for ct in range(16):
        dst = (t.R1[:, ct * 2048:(ct + 1) * 2048] if ct < 8
               else t.SP[:, (ct - 8) * 2048:(ct - 7) * 2048])
        src = t.w_in[wname].rearrange("(a p) m -> p a m", p=128)[:, ct:ct + 1, :]
        scrp = t.wb[wname].rearrange("(a p) m -> p a m", p=128)[:, ct:ct + 1, :]
        tok = wc.stream(bld, dst, src, scrp, f"{wname}:{ct}", waits=guard)
        slab_toks.append(tok)

    def wo_slab(ct, c0, c1):
        if ct < 8:
            return t.R1[:, ct * 2048 + c0: ct * 2048 + c1]
        return t.SP[:, (ct - 8) * 2048 + c0:(ct - 8) * 2048 + c1]

    def ctx_ap(ct, tt):
        half = tt // 8
        q0 = (tt % 8) * 128
        if ct < 8:
            base = ct * 2048 + half * 1024 + q0
            return t.R2[:, base:base + 128]
        base = 16384 + (ct - 8) * 2048 + half * 1024 + q0
        return t.R1[:, base:base + 128]

    boe_bc = t.kb_boe
    bv_col = t.cb_bvA if stage == "A" else t.cb_bvB
    if b == 0:
        w = [t.tok_consts[2 if stage == "A" else 3]] + guard

        def bvcv(eng):
            return eng.copy(out=t.kb_bvtmp, in_=bv_col)
        tok_bv = bld.st("scalar", bvcv, waits=w, tag=f"{stage}:bvcv")
        w = list(slab_toks) + [tok_bv]

        def boe(eng):
            ins = None
            for ct in range(16):
                for c0 in range(0, 2048, 512):
                    ins = eng.matmul(t.ps_a[0:1, c0:c0 + 512],
                                     t.kb_bvtmp[:, ct:ct + 1],
                                     wo_slab(ct, c0, c0 + 512),
                                     start=(ct == 0), stop=(ct == 15))
            return ins
        tok_b1 = bld.st("tensor", boe, waits=w, tag=f"{stage}:boeff")
        if stage == "A":
            tok_bo = bld.dma("gpsimd", t.fp_borow, t.bo_A_in, waits=guard, tag="c:boA")

            def boadd(eng):
                return eng.tensor_add(out=t.fp_borow, in0=t.ps_a[0:1, 0:2048],
                                      in1=t.fp_borow)
            tok_b2 = bld.st("vector", boadd, waits=[tok_b1, tok_bo], drain=True)
        else:
            def bocp(eng):
                return eng.tensor_copy(out=t.fp_borow, in_=t.ps_a[0:1, 0:2048])
            tok_b2 = bld.st("vector", bocp, waits=[tok_b1], drain=True)
        idx = 0 if stage == "A" else 1
        tok_b3 = bld.dma("gpsimd", t.boe_scr[idx:idx + 1, :], t.fp_borow, waits=[tok_b2])
        tok_b4 = bld.dma("gpsimd", t.fp_t1,
                         bass.AP(t.boe_scr.tensor, idx * D, [[0, 128], [1, D]]),
                         waits=[tok_b3])

        def bcv(eng):
            return eng.copy(out=boe_bc, in_=t.fp_t1)
        tok_boe = bld.st("scalar", bcv, waits=[tok_b4], tag=f"{stage}:boebc")
    else:
        idx = 0 if stage == "A" else 1
        tok_boe = load_bc_bf16(bld, boe_bc, t.boe_scr[idx:idx + 1, :], t.fp_t1,
                               guard, tag=f"{stage}boe")

    xb_pair = (t.fp_t2, t.fp_t3)
    ab = t.fp_t1
    tok_ps = [None, None]
    tok_ab_rd = None
    tok_xb_rd = [None, None]
    dg_guard = a.tok("dg")

    for tt in range(NT):
        ps = (t.ps_a, t.ps_b)[tt % 2]
        w = list(slab_toks) + [t.tok_attn[0]] + ([tok_ps[tt % 2]] if tok_ps[tt % 2] else [])
        if b == 0 and tt == 0:
            w.append(tok_boe)

        def wmm(eng, tt=tt, ps=ps):
            ins = None
            for ct in range(16):
                for oc in range(4):
                    ins = eng.matmul(ps[:, oc * 512:(oc + 1) * 512],
                                     ctx_ap(ct, tt),
                                     wo_slab(ct, oc * 512, (oc + 1) * 512),
                                     start=(ct == 0), stop=(ct == 15))
            return ins
        tok_mm = bld.st("tensor", wmm, waits=w, tag=f"{stage}wo:mm{tt}")

        xb = xb_pair[tt % 2]
        w = [dg_guard] + guard + ([tok_xb_rd[tt % 2]] if tok_xb_rd[tt % 2] else [])
        src = (t.x[b, tt * 128:(tt + 1) * 128, :] if stage == "A"
               else t.x2_scr[b, tt * 128:(tt + 1) * 128, :])
        tok_x = bld.dma("gpsimd", xb, src, waits=w, tag=f"{stage}wo:x{tt}")

        if stage == "A":
            w = [tok_mm, tok_boe] + ([tok_ab_rd] if tok_ab_rd else [])

            def ev(eng, ps=ps):
                return eng.tensor_add(out=ab, in0=ps[:], in1=boe_bc)
            tok_ev = bld.st("vector", ev, waits=w, drain=True, tag=f"Awo:ev{tt}")
        else:
            m2s = t.fp_m2[:, tt:tt + 1]
            r2s = t.fp_r2[:, tt:tt + 1]
            nb = t.fp_st[:, 28:29]

            def negb(eng, m2s=m2s, r2s=r2s, nb=nb):
                return eng.tensor_scalar(out=nb, in0=m2s, scalar1=r2s, scalar2=-1.0,
                                         op0=ALU.mult, op1=ALU.mult)
            tok_nb = bld.st("vector", negb, waits=[tok_x], drain=True)

            def zap(eng, xb=xb, r2s=r2s, nb=nb):
                return eng.activation(out=xb, in_=xb, func=AF.Identity, bias=nb, scale=r2s)
            tok_z = bld.st("scalar", zap, waits=[tok_nb], tag=f"Bwo:z{tt}")
            w = [tok_z, tok_mm, tok_boe] + ([tok_ab_rd] if tok_ab_rd else [])

            def ev(eng, ps=ps, xb=xb):
                eng.tensor_add(out=ab, in0=ps[:], in1=xb)
                eng.drain()
                return eng.tensor_add(out=ab, in0=ab, in1=boe_bc)
            tok_ev = bld.st("vector", ev, waits=w, drain=True, tag=f"Bwo:ev{tt}")
            tok_xin = bld.dma("gpsimd", t.xin_scr[tt * 128:(tt + 1) * 128, :], ab,
                              waits=[tok_ev], tag=f"Bwo:xin{tt}")
        tok_ps[tt % 2] = tok_ev

        tok_bn = ln_stats(bld, ab, 2048, [tok_ev], tag=f"{stage}wo.{tt}")

        if stage == "A":
            tok_sc, rstd, nmr = ln_scale(bld, tok_bn, EPS_LN, tag=f"Awo.{tt}")
            w = [tok_sc, tok_x]

            def x2c(eng, xb=xb, rstd=rstd, nmr=nmr):
                eng.tensor_scalar(out=xb, in0=xb, scalar1=nmr, scalar2=None,
                                  op0=ALU.add)
                eng.drain()
                return eng.scalar_tensor_tensor(out=xb, in0=ab, scalar=rstd, in1=xb,
                                                op0=ALU.mult, op1=ALU.add)
            tok_x2 = bld.st("vector", x2c, waits=w, drain=True, tag=f"Awo:x2.{tt}")
            tok_ab_rd = tok_x2
            tok_bn2 = ln_stats(bld, xb, 2048, [tok_x2], tag=f"Ax2.{tt}")
            mv = t.fp_st[:, 24:26]
            r2d = t.fp_r2[:, tt:tt + 1]
            m2d = t.fp_m2[:, tt:tt + 1]

            def sq2(eng, mv=mv, r2d=r2d):
                return eng.activation(out=r2d, in_=mv[:, 1:2], func=AF.Sqrt, bias=EPS_LN)
            tok_sq2 = bld.st("scalar", sq2, waits=[tok_bn2])

            std = t.fp_st[:, 29:30]
            s1d = t.fp_s1[:, tt:tt + 1]

            def sq3(eng, mv=mv, std=std):
                return eng.activation(out=std, in_=mv[:, 1:2], func=AF.Sqrt,
                                      bias=EPS_LN * EPS_RMS, scale=1.0 + EPS_RMS)
            tok_sq3 = bld.st("scalar", sq3, waits=[tok_bn2])

            def fin2(eng, mv=mv, r2d=r2d, m2d=m2d, std=std, s1d=s1d):
                eng.reciprocal(out=r2d, in_=r2d)
                eng.reciprocal(out=s1d, in_=std)
                eng.drain()
                return eng.tensor_copy(out=m2d, in_=mv[:, 0:1])
            tok_f2 = bld.st("vector", fin2, waits=[tok_sq2, tok_sq3], drain=True)
            tok_w = bld.dma("gpsimd", t.x2_scr[b, tt * 128:(tt + 1) * 128, :], xb,
                            waits=[tok_x2, tok_f2], tag=f"Awo:x2w{tt}")
            tok_xb_rd[tt % 2] = tok_w
            if _OUT_STAGE == "x2":
                bld.dma("gpsimd", t.out[b, tt * 128:(tt + 1) * 128, :], xb,
                        waits=[tok_x2, tok_f2])
        else:
            mv = t.fp_st[:, 24:26]
            rstd = t.fp_st[:, 26:27]
            rrd = t.fp_rr[:, tt:tt + 1]

            def msq(eng, mv=mv, rstd=rstd):
                return eng.scalar_tensor_tensor(out=rstd, in0=mv[:, 0:1],
                                                scalar=mv[:, 0:1], in1=mv[:, 1:2],
                                                op0=ALU.mult, op1=ALU.add)
            tok_m = bld.st("vector", msq, waits=[tok_bn], drain=True)

            def sq(eng, rstd=rstd):
                return eng.activation(out=rstd, in_=rstd, func=AF.Sqrt, bias=EPS_RMS)
            tok_sq = bld.st("scalar", sq, waits=[tok_m], tag=f"Bwo:rms{tt}")

            def rcp(eng, rstd=rstd, rrd=rrd):
                eng.reciprocal(out=rrd, in_=rstd)
                eng.drain()
                return eng.engine_nop()
            tok_r = bld.st("vector", rcp, waits=[tok_sq], drain=True)
            tok_ab_rd = tok_xin
            tok_xb_rd[tt % 2] = tok_ev
    t.tok_wo = (a.tok("ve"), a.tok("se"), a.tok("pe"), a.tok("dg"))


def phase_router(bld, b):
    t = bld.t
    a = bld.asm
    wc = t.wc
    guard = [t.tok_wo[2], t.tok_wo[0], t.tok_wo[1]]
    dg_guard = t.tok_wo[3]

    htm = (t.SP[:, 0:2048], t.SP[:, 2048:4096])
    xb2 = (t.fp_t1, t.fp_t2)
    tok_ln = load_bc_bf16(bld, t.kb_ln, t.w_ln2_in, t.fp_t1,
                          guard + [dg_guard], tag="ln2")
    tok_ps = [None, None]
    tok_htm = [None, None]
    tok_xb = [None, None]
    for tt in range(NT):
        xb = xb2[tt % 2]
        w = [dg_guard, tok_ln] + guard + ([tok_xb[tt % 2]] if tok_xb[tt % 2] else [])
        tok_x = bld.dma("gpsimd", xb, t.xin_scr[tt * 128:(tt + 1) * 128, :], waits=w,
                        tag=f"hp:x{tt}")
        rr = t.fp_rr[:, tt:tt + 1]
        hb = htm[tt % 2]
        w = [tok_x] + guard + ([tok_htm[tt % 2]] if tok_htm[tt % 2] else [])

        def ap1(eng, xb=xb, rr=rr):
            return eng.activation(out=xb, in_=xb, func=AF.Identity, scale=rr)
        tok_a1 = bld.st("scalar", ap1, waits=w, tag=f"hp:a{tt}")

        def ap2(eng, xb=xb, hb=hb):
            return eng.tensor_mul(out=hb, in0=xb, in1=t.kb_ln)
        tok_h = bld.st("vector", ap2, waits=[tok_a1, tok_ln], drain=True,
                       tag=f"hp:m{tt}")
        tok_xb[tt % 2] = tok_h
        ps = (t.ps_a, t.ps_b)[tt % 2]
        w = [tok_h] + ([tok_ps[tt % 2]] if tok_ps[tt % 2] else []) + guard[:1]

        def tr(eng, hb=hb, ps=ps):
            ins = None
            for k in range(16):
                ins = eng.matmul(ps[:, k * 128:(k + 1) * 128],
                                 hb[:, k * 128:(k + 1) * 128], t.kb_ident,
                                 start=True, stop=True)
            return ins
        tok_tr = bld.st("tensor", tr, waits=w, tag=f"hp:tr{tt}")
        tok_htm[tt % 2] = tok_tr
        dst = t.R1[:].rearrange("p (a m) -> p a m", a=16)[:, :, tt * 128:(tt + 1) * 128]

        def ev(eng, ps=ps, dst=dst):
            return eng.tensor_copy(out=dst, in_=ps[:].rearrange("p (a m) -> p a m", a=16))
        tok_ev = bld.st("vector", ev, waits=[tok_tr], drain=True, tag=f"hp:ev{tt}")
        tok_ps[tt % 2] = tok_ev
    t.tok_hpT = a.tok("ve")

    slab = t.SP[:, 4096:4240]
    key = "wre"
    if key not in wc.seen:
        wc.seen.add(key)
        fb = t.fp_conv[0]
        w = [wc.fb_tok[0]] if wc.fb_tok[0] else []
        tok_l1 = bld.dma("sync", fb[:, 0:128].rearrange("p (a m) -> p a m", m=8),
                         _r16(t.w_in["w_router"]), waits=w, tag="wre:l1")
        tok_l2 = bld.dma("sync", fb[:, 128:144].rearrange("p (a m) -> p a m", m=1),
                         _r16(t.w_in["w_sh_eg"]), tag="wre:l2", slow=True)

        def cv(eng, fb=fb, slab=slab):
            eng.copy(out=slab.rearrange("p (a m) -> p a m", m=9)[:, :, 0:8],
                     in_=fb[:, 0:128].rearrange("p (a m) -> p a m", m=8))
            return eng.copy(out=slab.rearrange("p (a m) -> p a m", m=9)[:, :, 8:9],
                            in_=fb[:, 128:144].rearrange("p (a m) -> p a m", m=1))
        tok_cv = bld.st("scalar", cv, waits=[tok_l1, tok_l2], tag="wre:cv")
        wc.fb_tok[0] = tok_cv
        bld.dma("gpsimd", _r16(t.wb["wre"]),
                slab.rearrange("p (a m) -> p a m", m=9), waits=[tok_cv])
        tok_wre = tok_cv
    else:
        tok_wre = bld.dma("sync", slab.rearrange("p (a m) -> p a m", m=9),
                          _r16(t.wb["wre"]), waits=guard, tag="wre:ld")

    lg = t.fp_st[:, 0:16]
    tok_ps2 = [None, None]
    tok_lg = None
    for tt in range(NT):
        ps = (t.ps_a, t.ps_b)[tt % 2]
        w = [t.tok_hpT, tok_wre] + ([tok_ps2[tt % 2]] if tok_ps2[tt % 2] else [])

        def rmm(eng, tt=tt, ps=ps):
            ins = None
            for kt in range(16):
                ins = eng.matmul(ps[:, 0:9],
                                 t.R1[:, kt * 2048 + tt * 128: kt * 2048 + (tt + 1) * 128],
                                 slab[:, kt * 9:(kt + 1) * 9],
                                 start=(kt == 0), stop=(kt == 15))
            return ins
        tok_mm = bld.st("tensor", rmm, waits=w, tag=f"rt:mm{tt}")
        w = [tok_mm] + ([tok_lg] if tok_lg else [])

        def sfm(eng, ps=ps):
            return eng.reduce_max(out=lg[:, 8:9], in_=ps[:, 0:8], axis=AX, negate=True)
        tok_m = bld.st("vector", sfm, waits=w, drain=True)

        def exf(eng, ps=ps):
            return eng.activation(out=lg[:, 0:8], in_=ps[:, 0:8], func=AF.Exp,
                                  bias=lg[:, 8:9], scale=1.0, accum_out=lg[:, 9:10])
        tok_e = bld.st("scalar", exf, waits=[tok_m], tag=f"rt:ex{tt}")

        def sgm(eng, ps=ps):
            return eng.activation(out=lg[:, 10:11], in_=ps[:, 8:9], func=AF.Sigmoid)
        tok_g = bld.st("scalar", sgm, waits=[tok_e], tag=f"rt:sg{tt}")
        tok_ps2[tt % 2] = tok_g
        cdst = t.c_all[:, tt * 9:(tt + 1) * 9]

        def top2(eng, cdst=cdst):
            eng.reciprocal(out=lg[:, 11:12], in_=lg[:, 9:10])
            eng.drain()
            eng.tensor_scalar(out=lg[:, 0:8], in0=lg[:, 0:8], scalar1=lg[:, 11:12],
                              scalar2=None, op0=ALU.mult)
            eng.drain()
            eng.reduce_max(out=lg[:, 8:9], in_=lg[:, 0:8], axis=AX)
            eng.drain()
            eng.tensor_scalar(out=cdst[:, 0:8], in0=lg[:, 0:8], scalar1=lg[:, 8:9],
                              scalar2=None, op0=ALU.is_ge)
            eng.drain()
            eng.scalar_tensor_tensor(out=lg[:, 0:8], in0=cdst[:, 0:8], scalar=-2.0,
                                     in1=lg[:, 0:8], op0=ALU.mult, op1=ALU.add)
            eng.drain()
            eng.reduce_max(out=lg[:, 8:9], in_=lg[:, 0:8], axis=AX)
            eng.drain()
            eng.scalar_tensor_tensor(out=lg[:, 0:8], in0=cdst[:, 0:8], scalar=2.0,
                                     in1=lg[:, 0:8], op0=ALU.mult, op1=ALU.add)
            eng.drain()
            eng.tensor_scalar(out=cdst[:, 0:8], in0=lg[:, 0:8], scalar1=lg[:, 8:9],
                              scalar2=None, op0=ALU.is_ge)
            eng.drain()
            eng.tensor_mul(out=cdst[:, 0:8], in0=cdst[:, 0:8], in1=lg[:, 0:8])
            eng.drain()
            return eng.tensor_copy(out=cdst[:, 8:9], in_=lg[:, 10:11])
        tok_lg = bld.st("vector", top2, waits=[tok_g], drain=True, tag=f"rt:t2.{tt}")
    t.tok_router = a.tok("ve")


def phase_moe(bld, b):
    t = bld.t
    a = bld.asm
    wc = t.wc
    acc = t.fp_acc
    wg = (t.SP[:, 0:2048], t.SP[:, 2048:4096])
    wd_pair = (t.SP[:, 4096:9728], t.SP[:, 9728:15360])
    wu = (t.SP[:, 15360:17408], t.SP[:, 17408:19456])
    sGb = t.R2[:, 11264:11776]
    uCb = t.R2[:, 11776:12288]
    aT = (t.R2[:, 0:5632], t.R2[:, 5632:11264])

    guard = [a.tok("pe"), t.tok_router, a.tok("se")]
    tok_wg = [None, None]
    tok_wu = [None, None]
    tok_wd = [None, None]
    tok_aT_rd = [None, None]
    tok_g_ps = [None, None]
    tok_u_ps = [None, None]
    tok_d_ps = [None] * 4
    tok_acc = [None] * 4
    tok_sG = None
    tok_uC = None
    tok_am = None
    tok_fin_x = None

    def gu_block(e_or_sec, it, kt_base_key, wgsrc, wgscr, wusrc, wuscr, ch_lo, apar, tag):
        nonlocal tok_sG, tok_uC, tok_am
        wpar = it % 2
        w = guard + ([tok_wg[wpar]] if tok_wg[wpar] else [])
        tkg = wc.stream(bld, wg[wpar], wgsrc, wgscr, f"wg{kt_base_key}", waits=w)
        w = guard + ([tok_wu[wpar]] if tok_wu[wpar] else [])
        tku = wc.stream(bld, wu[wpar], wusrc, wuscr, f"wu{kt_base_key}", waits=w)
        gps = t.ps_a[:, (it % 2) * 512:(it % 2) * 512 + 512]
        ups = t.ps_a[:, (2 + it % 2) * 512:(2 + it % 2) * 512 + 512]
        w = [tkg, t.tok_hpT] + ([tok_g_ps[it % 2]] if tok_g_ps[it % 2] else [])

        def gmm(eng, wg_=wg[wpar], gps=gps, ch_lo=ch_lo):
            ins = None
            for kt in range(16):
                ins = eng.matmul(gps, wg_[:, kt * 128:(kt + 1) * 128],
                                 t.R1[:, kt * 2048 + ch_lo: kt * 2048 + ch_lo + TC],
                                 start=(kt == 0), stop=(kt == 15))
            return ins
        tok_g = bld.st("tensor", gmm, waits=w, tag=f"{tag}:g")
        tok_wg[wpar] = tok_g
        w = [tku, t.tok_hpT] + ([tok_u_ps[it % 2]] if tok_u_ps[it % 2] else [])

        def umm(eng, wu_=wu[wpar], ups=ups, ch_lo=ch_lo):
            ins = None
            for kt in range(16):
                ins = eng.matmul(ups, wu_[:, kt * 128:(kt + 1) * 128],
                                 t.R1[:, kt * 2048 + ch_lo: kt * 2048 + ch_lo + TC],
                                 start=(kt == 0), stop=(kt == 15))
            return ins
        tok_u = bld.st("tensor", umm, waits=w, tag=f"{tag}:u")
        tok_wu[wpar] = tok_u
        w = [tok_g] + ([tok_sG] if tok_sG else [])

        def esg(eng, gps=gps):
            return eng.activation(out=sGb, in_=gps, func=AF.Silu)
        tok_sg = bld.st("scalar", esg, waits=w, tag=f"{tag}:sg")
        tok_g_ps[it % 2] = tok_sg
        w = [tok_u] + ([tok_uC] if tok_uC else [])

        def euc(eng, ups=ups):
            return eng.copy(out=uCb, in_=ups)
        tok_uc = bld.st("scalar", euc, waits=w, tag=f"{tag}:uc")
        tok_u_ps[it % 2] = tok_uc
        adst = aT[apar][:, it * 512:(it + 1) * 512]
        w = [tok_sg, tok_uc]
        if it == 0 and tok_aT_rd[apar]:
            w.append(tok_aT_rd[apar])
        if tok_am:
            w.append(tok_am)

        def amul(eng, adst=adst):
            return eng.tensor_mul(out=adst, in0=sGb, in1=uCb)
        tok_am = bld.st("vector", amul, waits=w, drain=True, tag=f"{tag}:am")
        tok_sG = tok_am
        tok_uC = tok_am
        return tok_am

    def down_block(oc, wdsrc, wdscr, key, tok_a_done, ch, cidx, tag):
        wpar = oc % 2
        w = guard + ([tok_wd[wpar]] if tok_wd[wpar] else [])
        tkd = wc.stream(bld, wd_pair[wpar], wdsrc, wdscr, key, waits=w)
        last = None
        for tt in range(4):
            dps = t.ps_b[:, tt * 512:(tt + 1) * 512]
            w = [tkd, tok_a_done] + ([tok_d_ps[tt]] if tok_d_ps[tt] else [])
            apar = cidx[0]

            def dmm(eng, wd_=wd_pair[wpar], dps=dps, tt=tt, apar=apar):
                ins = None
                for it in range(NIT):
                    ins = eng.matmul(dps,
                                     aT[apar][:, it * 512 + tt * 128: it * 512 + (tt + 1) * 128],
                                     wd_[:, it * 512:(it + 1) * 512],
                                     start=(it == 0), stop=(it == NIT - 1))
                return ins
            tok_d = bld.st("tensor", dmm, waits=w, tag=f"{tag}:d{tt}")
            tok_wd[wpar] = tok_d
            last = tok_d
            cap = t.c_all[:, (ch * 4 + tt) * 9 + cidx[1]:(ch * 4 + tt) * 9 + cidx[1] + 1]
            accs = acc[:, tt * 2048 + oc * 512: tt * 2048 + (oc + 1) * 512]
            w = [tok_d, t.tok_router] + ([tok_acc[tt]] if tok_acc[tt] else [])
            if cidx[2]:
                def ac(eng, dps=dps, cap=cap, accs=accs):
                    return eng.tensor_scalar(out=accs, in0=dps, scalar1=cap,
                                             scalar2=None, op0=ALU.mult)
            else:
                def ac(eng, dps=dps, cap=cap, accs=accs):
                    return eng.scalar_tensor_tensor(out=accs, in0=dps, scalar=cap,
                                                    in1=accs, op0=ALU.mult, op1=ALU.add)
            tok_a2 = bld.st("vector", ac, waits=w, drain=True, tag=f"{tag}:ac{tt}")
            tok_d_ps[tt] = tok_a2
            tok_acc[tt] = tok_a2
        return last

    for ch in range(NCH):
        ch_lo = ch * TC
        for e in range(N_EXP):
            apar = e % 2
            tok_a_done = None
            for it in range(NIT):
                tok_a_done = gu_block(
                    e, it, f"G:{e}:{it}",
                    _rslab(t.w_in["w_gate_e"][e], it), _rslab(t.wb["w_gate_e"][e], it),
                    _rslab(t.w_in["w_up_e"][e], it), _rslab(t.wb["w_up_e"][e], it),
                    ch_lo, apar, f"m{e}.{it}")
            tok_dlast = None
            for oc in range(4):
                tok_dlast = down_block(
                    oc, _rslab(t.w_in["w_down_e"][e], oc, 512),
                    _rslab(t.wb["w_down_e"][e], oc, 512),
                    f"wd:{e}:{oc}", tok_a_done, ch, (apar, e, e == 0),
                    f"m{e}")
            tok_aT_rd[apar] = tok_dlast
        for sec in range(4):
            apar = sec % 2
            tok_a_done = None
            for it in range(NIT):
                git = sec * NIT + it
                tok_a_done = gu_block(
                    sec, it, f"S:{git}",
                    _rslab(t.w_in["w_sh_gate"], git), _rslab(t.wb["w_sh_gate"], git),
                    _rslab(t.w_in["w_sh_up"], git), _rslab(t.wb["w_sh_up"], git),
                    ch_lo, apar, f"s{sec}.{it}")
            base = sec * NIT * 128
            tok_dlast = None
            for oc in range(4):
                tok_dlast = down_block(
                    oc, _rslab(t.w_in["w_sh_down"][base:base + NIT * 128, :], oc, 512),
                    _rslab(t.wb["w_sh_down"][base:base + NIT * 128, :], oc, 512),
                    f"wsd:{sec}:{oc}", tok_a_done, ch, (apar, 8, False), f"s{sec}")
            tok_aT_rd[apar] = tok_dlast

        for tt in range(4):
            gt = ch * 4 + tt
            xi = t.fp_t1
            w = [tok_acc[tt]] + ([tok_fin_x] if tok_fin_x else [])
            tok_xi = bld.dma("gpsimd", xi, t.xin_scr[gt * 128:(gt + 1) * 128, :],
                             waits=w, tag=f"fin:xi{gt}")
            accs = acc[:, tt * 2048:(tt + 1) * 2048]

            def vadd(eng, accs=accs, xi=xi):
                return eng.tensor_add(out=accs, in0=accs, in1=xi)
            tok_v = bld.st("vector", vadd, waits=[tok_xi, tok_acc[tt]], drain=True,
                           tag=f"fin:v{gt}")
            tok_bn = ln_stats(bld, accs, 2048, [tok_v], tag=f"fin.{gt}")
            tok_sc, rstd, nmr = ln_scale(bld, tok_bn, EPS_LN, tag=f"fin.{gt}")
            tok_x2 = bld.dma("gpsimd", xi, t.x2_scr[b, gt * 128:(gt + 1) * 128, :],
                             waits=[tok_v], tag=f"fin:x2{gt}")

            def fino(eng, accs=accs, xi=xi, rstd=rstd, nmr=nmr):
                eng.tensor_scalar(out=xi, in0=xi, scalar1=nmr, scalar2=None,
                                  op0=ALU.add)
                eng.drain()
                return eng.scalar_tensor_tensor(out=xi, in0=accs, scalar=rstd, in1=xi,
                                                op0=ALU.mult, op1=ALU.add)
            tok_f = bld.st("vector", fino, waits=[tok_sc, tok_x2], drain=True,
                           tag=f"fin:o{gt}")
            tok_fin_x = bld.dma("gpsimd", t.out[b, gt * 128:(gt + 1) * 128, :], xi,
                                waits=[tok_f], tag=f"fin:out{gt}")
            tok_acc[tt] = tok_f


def kernel(**inputs):
    xs = np.ascontiguousarray(inputs["L0_attn_x"])
    consts = _consts()
    base = {
        "Wq_A": inputs["L0_attn_Wq"], "Wk_A": inputs["L0_attn_Wk"],
        "Wv_A": inputs["L0_attn_Wv"], "Wo_A": inputs["L0_attn_Wo"],
        "wq_B": inputs["L0_ffn_wq"], "wk_B": inputs["L0_ffn_wk"],
        "wv_B": inputs["L0_ffn_wv"], "wo_B": inputs["L0_ffn_wo"],
        "w_router": inputs["L0_ffn_w_router"],
        "w_gate_e": inputs["L0_ffn_w_gate_e"], "w_up_e": inputs["L0_ffn_w_up_e"],
        "w_down_e": inputs["L0_ffn_w_down_e"],
        "w_sh_gate": inputs["L0_ffn_w_sh_gate"], "w_sh_up": inputs["L0_ffn_w_sh_up"],
        "w_sh_down": inputs["L0_ffn_w_sh_down"], "w_sh_eg": inputs["L0_ffn_w_sh_eg"],
        "bq_Ar": inputs["L0_attn_bq"].reshape(1, 2048),
        "bk_Ar": inputs["L0_attn_bk"].reshape(1, 128),
        "bv_Acol": np.tile(inputs["L0_attn_bv"], 16).reshape(16, 128).T,
        "bo_A": inputs["L0_attn_bo"].reshape(1, 2048),
        "bq_Br": inputs["L0_ffn_bq"].reshape(1, 2048),
        "bk_Br": inputs["L0_ffn_bk"].reshape(1, 2048),
        "bv_Bcol": inputs["L0_ffn_bv"].reshape(16, 128).T,
        "w_ln1": inputs["L0_ffn_w_ln1"], "w_ln2": inputs["L0_ffn_w_ln2"],
        "ident": consts["ident"], "cmask": consts["cmask"],
        "ropeA": consts["ropeA"], "ropeB": consts["ropeB"],
    }
    base = {k: np.ascontiguousarray(v) for k, v in base.items()}
    in_maps = []
    for c in range(8):
        m = dict(base)
        m["x"] = np.ascontiguousarray(xs[c * BC:(c + 1) * BC])
        in_maps.append(m)
    nc = build_program()
    res = run_bass_kernel_spmd(nc, in_maps, core_ids=list(range(8)))
    out = np.concatenate([r["out"] for r in res.results], axis=0)
    return np.ascontiguousarray(out.astype(np.float32))


# version 02
# speedup vs baseline: 1.1049x; 1.1049x over previous
import numpy as np
import ml_dtypes
import concourse.bass as bass
import concourse.mybir as mybir
from concourse.bass_utils import run_bass_kernel_spmd

F32 = mybir.dt.float32
BF16 = mybir.dt.bfloat16
AF = mybir.ActivationFunctionType
ALU = mybir.AluOpType
AX = mybir.AxisListType.X

BC = 2
T = 2048
D = 2048
NT = 16
NH = 16
HD = 128
MOE_I = 1408
NIT = 11
SH_I = 5632
N_EXP = 8
TC = 512
NCH = T // TC
EPS_LN = 1e-5
EPS_RMS = 1e-6
NEG_BIG = -3.0e37

PH_ALL = ["A1", "A2", "A3", "A4", "B2", "B3", "B4", "B5", "B6"]
_BUILD_PHASES = list(PH_ALL)
_OUT_STAGE = "x3"
_LAST_ASM = None


class Asm:
    def __init__(self):
        self.steps = []
        self.counts = {}

    def add_sem(self, name):
        self.counts[name] = 0

    def step(self, eng, fn, waits=(), incs=(), drain=False, tag=""):
        for s, n in incs:
            self.counts[s] += n
        self.steps.append((eng, fn, tuple(w for w in waits if w is not None),
                           tuple(incs), drain, tag))

    def tok(self, sem):
        return (sem, self.counts[sem])


class Builder:
    ENG_SEM = {"tensor": "pe", "vector": "ve", "scalar": "se", "gpsimd": "ge",
               "sync": "sy"}
    DMA_SEM = {"sync": "dq", "gpsimd": "dg", "scalar": "ds"}

    def __init__(self, nc, asm, t):
        self.nc = nc
        self.asm = asm
        self.t = t

    def st(self, eng, fn, waits=(), drain=False, tag=""):
        sem = self.ENG_SEM[eng]
        self.asm.step(eng, fn, waits=waits, incs=[(sem, 1)], drain=drain, tag=tag)
        return self.asm.tok(sem)

    def dma(self, queue, out_ap, in_ap, waits=(), tag="", slow=False):
        sem = self.DMA_SEM[queue]

        def fn(eng, out_ap=out_ap, in_ap=in_ap, slow=slow):
            return eng.dma_start(out=out_ap, in_=in_ap,
                                 allow_slow_non_contiguous=slow)

        self.asm.step(queue, fn, waits=waits, incs=[(sem, 16)], tag=tag)
        return self.asm.tok(sem)

    def fence(self, engs=("pe", "ve", "se")):
        return [self.asm.tok(e) for e in engs]


class WCache:

    def __init__(self):
        self.seen = set()
        self.fb_tok = [None, None]
        self.par = 0
        self.buf_save = {}

    def stream(self, bld, dst, src32, scr16, key, waits=()):
        bkey = (dst.tensor.name, dst.offset)
        w_extra = [self.buf_save.pop(bkey)] if bkey in self.buf_save else []
        if key in self.seen:
            return bld.dma("sync", _match(dst, scr16), scr16,
                           waits=list(waits) + w_extra, tag=f"w${key}")
        self.seen.add(key)
        t = bld.t
        n = dst.shape[-1] if len(dst.shape) == 2 else None
        assert n is not None
        fw = 1024
        tok_c = None
        tok_s = None
        for i in range((n + fw - 1) // fw):
            lo = i * fw
            hi = min(n, lo + fw)
            p = self.par % 2
            self.par += 1
            fb = t.fp_conv[p][:, 0:hi - lo]
            srcp = _ccols(src32, lo, hi)
            w = list(waits) + w_extra + ([self.fb_tok[p]] if self.fb_tok[p] else [])
            w_extra = []
            tok_l = bld.dma("sync", _match(fb, srcp), srcp, waits=w, tag=f"wl${key}")

            def cv(eng, fb=fb, dst=dst, lo=lo, hi=hi):
                return eng.copy(out=dst[:, lo:hi], in_=fb)

            tok_c = bld.st("scalar", cv, waits=[tok_l], tag=f"wc${key}")
            self.fb_tok[p] = tok_c
            scrp = _ccols(scr16, lo, hi)
            tok_s = bld.dma("gpsimd", scrp, _match(dst[:, lo:hi], scrp),
                            waits=[tok_c], tag=f"ws${key}")
        self.buf_save[bkey] = tok_s
        return tok_c


def _rope_tables(theta, scale):
    i = np.arange(64, dtype=np.float64)
    inv = theta ** (-i / 64.0)
    pos = np.arange(T, dtype=np.float64)
    ang = pos[:, None] * inv[None, :]
    return (np.cos(ang) * scale).astype(np.float32), \
           (np.sin(ang) * scale).astype(np.float32)


def _consts():
    c = {}
    c["ident"] = np.eye(128, dtype=ml_dtypes.bfloat16)
    m = np.zeros((128, 128), dtype=np.float32)
    m[np.triu_indices(128, 1)] = NEG_BIG
    c["cmask"] = m
    sc = 1.0 / np.sqrt(128.0)
    cq, sq = _rope_tables(1e4, sc)
    ck, sk = _rope_tables(1e4, 1.0)
    c["ropeA"] = np.concatenate([cq, sq, ck, sk], axis=1)
    cq, sq = _rope_tables(1e6, sc)
    ck, sk = _rope_tables(1e6, 1.0)
    c["ropeB"] = np.concatenate([cq, sq, ck, sk], axis=1)
    return c


def _r16(ap):
    return ap.rearrange("(a p) m -> p a m", p=128)


def _rslab(ap, mt, msz=128):
    r = ap.rearrange("(a p) m -> p a m", p=128)
    return r[:, :, mt * msz:(mt + 1) * msz]


def _ccols(ap, lo, hi):
    if len(ap.shape) == 2:
        return ap[:, lo:hi]
    m = ap.shape[2]
    if ap.shape[1] == 1:
        return ap[:, :, lo:hi]
    assert lo % m == 0 and hi % m == 0, (lo, hi, m)
    return ap[:, lo // m:hi // m, :]


def _match(sb_flat, like):
    if len(like.shape) == 2:
        return sb_flat
    return sb_flat.rearrange("p (a m) -> p a m", m=like.shape[2])


def build_program():
    nc = bass.Bass("TRN2", target_bir_lowering=False)
    t = type("NS", (), {})()

    t.x = nc.dram_tensor("x", [BC, T, D], F32, kind="ExternalInput").ap()
    w_in = {}
    for name, shape in [
        ("Wq_A", [D, D]), ("Wk_A", [D, HD]), ("Wv_A", [D, HD]), ("Wo_A", [D, D]),
        ("wq_B", [D, D]), ("wk_B", [D, D]), ("wv_B", [D, D]), ("wo_B", [D, D]),
        ("w_router", [D, N_EXP]),
        ("w_gate_e", [N_EXP, D, MOE_I]), ("w_up_e", [N_EXP, D, MOE_I]),
        ("w_down_e", [N_EXP, MOE_I, D]),
        ("w_sh_gate", [D, SH_I]), ("w_sh_up", [D, SH_I]), ("w_sh_down", [SH_I, D]),
        ("w_sh_eg", [D, 1]),
    ]:
        w_in[name] = nc.dram_tensor(name, shape, F32, kind="ExternalInput").ap()
    t.w_in = w_in
    for nm, sh in [("bq_Ar", [1, D]), ("bk_Ar", [1, HD]), ("bv_Acol", [128, 16]),
                   ("bo_A", [1, D]), ("bq_Br", [1, D]), ("bk_Br", [1, D]),
                   ("bv_Bcol", [128, 16]), ("w_ln1", [D]), ("w_ln2", [D]),
                   ("cmask", [128, 128]), ("ropeA", [T, 256]), ("ropeB", [T, 256])]:
        setattr(t, nm + "_in", nc.dram_tensor(nm, sh, F32, kind="ExternalInput").ap())
    t.ident_in = nc.dram_tensor("ident", [128, 128], BF16, kind="ExternalInput").ap()
    t.identf_in = nc.dram_tensor("identf", [128, 128], F32, kind="ExternalInput").ap()
    t.onesr_in = nc.dram_tensor("onesr", [1, 128], F32, kind="ExternalInput").ap()
    t.out = nc.dram_tensor("out", [BC, T, D], F32, kind="ExternalOutput").ap()

    def scr(name, shape, dtype=BF16):
        return nc.dram_tensor(name, shape, dtype).ap()

    t.wb = {}
    for name, shape in [
        ("Wq_A", [D, D]), ("Wk_A", [D, HD]), ("Wv_A", [D, HD]), ("Wo_A", [D, D]),
        ("wq_B", [D, D]), ("wk_B", [D, D]), ("wv_B", [D, D]), ("wo_B", [D, D]),
        ("wre", [D, 9]),
        ("w_gate_e", [N_EXP, D, MOE_I]), ("w_up_e", [N_EXP, D, MOE_I]),
        ("w_down_e", [N_EXP, MOE_I, D]),
        ("w_sh_gate", [D, SH_I]), ("w_sh_up", [D, SH_I]), ("w_sh_down", [SH_I, D]),
    ]:
        t.wb[name] = scr("bf_" + name, shape)
    t.x2_scr = scr("x2_scr", [BC, T, D], F32)
    t.xin_scr = scr("xin_scr", [T, D], F32)
    t.qA_scr = scr("qA_scr", [NH, 128, NT, HD])
    t.qB_scr = scr("qB_scr", [NH, 128, NT, HD])
    t.kB_scr = scr("kB_scr", [NH, 128, NT, HD])
    t.vB_scr = scr("vB_scr", [NH, 128, NT, HD])
    t.boe_scr = scr("boe_scr", [2, D], F32)

    import contextlib
    stack = contextlib.ExitStack()
    sb = lambda n, c, d=BF16: stack.enter_context(nc.sbuf_tensor(n, [128, c], d))
    t.R1 = sb("R1", 32768)
    t.R2 = sb("R2", 16384)
    t.SP = sb("SP", 21504)
    t.FP = sb("FP", 14848, F32)
    t.CB = sb("CB", 512, F32)
    t.KB = sb("KB", 4496)
    t.ps_a = stack.enter_context(nc.psum_tensor("ps_a", [128, 2048], F32))
    t.ps_b = stack.enter_context(nc.psum_tensor("ps_b", [128, 2048], F32))

    FP = t.FP
    t.fp_tab = FP[:, 0:4096]
    t.fp_qtmp = (FP[:, 4096:4608], FP[:, 4608:5120])
    t.fp_rs = FP[:, 5120:5632]
    t.fp_bias = FP[:, 6144:8192]
    t.fp_borow = FP[0:1, 6144:8192]
    t.fp_acc = FP[:, 0:8192]
    t.fp_conv = (FP[:, 8192:9216], FP[:, 9216:10240])
    t.fp_sig = FP[:, 10240:10368]
    t.fp_st = FP[:, 10368:10560]
    t.c_all = FP[:, 10560:10704]
    t.fp_m2 = FP[:, 10704:10720]
    t.fp_r2 = FP[:, 10720:10736]
    t.fp_rr = FP[:, 10736:10752]
    t.fp_s1 = FP[:, 5632:5648]
    t.fp_t1 = FP[:, 10752:12800]
    t.fp_t2 = FP[:, 12800:14848]
    t.fp_t3 = FP[:, 0:2048]

    t.cb_cmask = t.CB[:, 0:128]
    t.cb_bvA = t.CB[:, 128:144]
    t.cb_bvB = t.CB[:, 144:160]
    t.cb_identf = t.CB[:, 256:384]
    t.cb_onesr = t.CB[0:1, 384:512]

    t.kb_ident = t.KB[:, 0:128]
    t.kb_diag = (t.KB[:, 128:256], t.KB[:, 256:384])
    t.kb_boe = t.KB[:, 384:2432]
    t.kb_ln = t.KB[:, 2432:4480]
    t.kb_bvtmp = t.KB[:, 4480:4496]

    t.const_vals = [EPS_LN, EPS_RMS, EPS_LN * EPS_RMS, 1.0 + EPS_RMS]
    for i, v in enumerate(t.const_vals):
        nc.const_aps.register(F32, v, t.CB[:, 200 + i:201 + i])

    sems = {}
    for s in ["pe", "ve", "se", "ge", "sy", "dq", "dg", "ds"]:
        sems[s] = stack.enter_context(nc.semaphore(s))
    t.sems = sems

    asm = Asm()
    for s in sems:
        asm.add_sem(s)
    bld = Builder(nc, asm, t)
    emit_all(bld)
    global _LAST_ASM
    _LAST_ASM = asm

    with stack:
        with nc.Block() as block:
            def make(engname):
                def closure(eng):
                    if engname != "gpsimd":
                        eng.wait_ge(sems["ge"], 1)
                    for (e, fn, waits, incs, drain, tag) in asm.steps:
                        if e != engname:
                            continue
                        for (s, v) in waits:
                            if v > 0:
                                eng.wait_ge(sems[s], v)
                        if drain:
                            eng.drain()
                        ins = fn(eng)
                        for (s, n) in incs:
                            ins.then_inc(sems[s], n)
                return closure
            block.sync(make("sync"))
            block.scalar(make("scalar"))
            block.vector(make("vector"))
            block.tensor(make("tensor"))
            block.gpsimd(make("gpsimd"))
    return nc


def emit_all(bld):
    t = bld.t
    a = bld.asm
    t.wc = WCache()

    def cmemset(eng):
        ins = None
        for i, v in enumerate(t.const_vals):
            ins = eng.memset(t.CB[:, 200 + i:201 + i], v)
        return ins
    t.tok_cm = bld.st("gpsimd", cmemset, tag="c:memset")
    toks = []
    toks.append(bld.dma("gpsimd", t.kb_ident, t.ident_in, tag="c:ident"))
    toks.append(bld.dma("gpsimd", t.cb_cmask, t.cmask_in, tag="c:cmask"))
    toks.append(bld.dma("gpsimd", t.cb_bvA, t.bv_Acol_in, tag="c:bvA"))
    toks.append(bld.dma("gpsimd", t.cb_bvB, t.bv_Bcol_in, tag="c:bvB"))
    toks.append(bld.dma("gpsimd", t.cb_identf, t.identf_in, tag="c:identf"))
    toks.append(bld.dma("gpsimd", t.cb_onesr, t.onesr_in, tag="c:onesr"))
    t.bcast = lambda ap, n: bass.AP(ap.tensor, ap.offset, [[0, 128], [1, n]])
    t.tok_consts = toks
    t.tok_ident = toks[0]

    for b in range(BC):
        t.b = b
        if "A1" in _BUILD_PHASES:
            phase_h_build(bld, b, src="x")
        if "A2" in _BUILD_PHASES:
            phase_QKV(bld, b, "A")
        if "A3" in _BUILD_PHASES:
            phase_attn(bld, b, "A")
        if "A4" in _BUILD_PHASES:
            phase_wo(bld, b, "A")
        if "B2" in _BUILD_PHASES:
            phase_hB(bld, b)
            phase_QKV(bld, b, "B")
        if "B3" in _BUILD_PHASES:
            phase_attn(bld, b, "B")
        if "B4" in _BUILD_PHASES:
            phase_wo(bld, b, "B")
        if "B5" in _BUILD_PHASES:
            phase_router(bld, b)
        if "B6" in _BUILD_PHASES:
            phase_moe(bld, b)

    def fin(eng):
        return eng.nop()
    a.step("gpsimd", fin, waits=[a.tok("dg"), a.tok("dq"), a.tok("ds")],
           incs=[("ge", 1)], tag="final")


def load_bc_bf16(bld, dst_bf16, src_row_ap, stage_fp, waits, tag=""):
    t = bld.t
    tok_l = bld.dma("gpsimd", stage_fp, t.bcast(src_row_ap, D), waits=waits,
                    tag="bc:" + tag)

    def cv(eng):
        return eng.copy(out=dst_bf16, in_=stage_fp)
    return bld.st("scalar", cv, waits=[tok_l], tag="bccv:" + tag)


def ln_stats(bld, src, n, waits, tag=""):
    t = bld.t
    stats6 = t.fp_st[:, 0:24]
    mv = t.fp_st[:, 24:26]

    def bn(eng, src=src, n=n):
        nblk = n // 512
        for i in range(nblk):
            eng.bn_stats(out=stats6[:, 6 * i:6 * i + 6], in_=src[:, 512 * i:512 * (i + 1)])
        eng.drain()
        return eng.bn_aggr(out=mv, in_=stats6[:, 0:6 * nblk])
    return bld.st("vector", bn, waits=waits, drain=True, tag="bn" + tag)


def ln_scale(bld, tok_bn, eps, tag=""):
    t = bld.t
    mv = t.fp_st[:, 24:26]
    rstd = t.fp_st[:, 26:27]
    nmr = t.fp_st[:, 27:28]

    def sq(eng):
        return eng.activation(out=rstd, in_=mv[:, 1:2], func=AF.Sqrt, bias=eps)
    tok_sq = bld.st("scalar", sq, waits=[tok_bn], tag="sq" + tag)

    def rcp(eng):
        eng.reciprocal(out=rstd, in_=rstd)
        eng.drain()
        return eng.tensor_scalar(out=nmr, in0=mv[:, 0:1], scalar1=rstd,
                                 scalar2=-1.0, op0=ALU.mult, op1=ALU.mult)
    return bld.st("vector", rcp, waits=[tok_sq], drain=True, tag="rcp" + tag), rstd, nmr


def phase_h_build(bld, b, src):
    t = bld.t
    a = bld.asm
    htm = (t.SP[:, 0:2048], t.SP[:, 2048:4096])
    xin = (t.fp_t1, t.fp_t2)
    guard = bld.fence(("pe", "ve", "se"))
    tok_x_rd = [None, None]
    tok_h_rd = [None, None]
    tok_ps = [None, None]

    for tt in range(NT):
        xb = xin[tt % 2]
        w = guard + ([tok_x_rd[tt % 2]] if tok_x_rd[tt % 2] else [])
        tok_x = bld.dma("gpsimd", xb, t.x[b, tt * 128:(tt + 1) * 128, :], waits=w,
                        tag=f"A1:x{tt}")
        tok_bn = ln_stats(bld, xb, 2048, [tok_x], tag=f"A1.{tt}")
        tok_sc, rstd, nmr = ln_scale(bld, tok_bn, EPS_LN, tag=f"A1.{tt}")
        hb = htm[tt % 2]
        w = [tok_sc] + ([tok_h_rd[tt % 2]] if tok_h_rd[tt % 2] else []) + guard[:1]

        def app(eng, xb=xb, hb=hb, rstd=rstd, nmr=nmr):
            return eng.activation(out=hb, in_=xb, func=AF.Identity, bias=nmr, scale=rstd)
        tok_app = bld.st("scalar", app, waits=w, tag=f"A1:app{tt}")
        tok_x_rd[tt % 2] = tok_app

        ps = (t.ps_a, t.ps_b)[tt % 2]
        w = [tok_app, t.tok_ident] + ([tok_ps[tt % 2]] if tok_ps[tt % 2] else [])

        def tr(eng, hb=hb, ps=ps):
            ins = None
            for k in range(16):
                ins = eng.matmul(ps[:, k * 128:(k + 1) * 128],
                                 hb[:, k * 128:(k + 1) * 128],
                                 t.kb_ident, start=True, stop=True)
            return ins
        tok_tr = bld.st("tensor", tr, waits=w, tag=f"A1:tr{tt}")
        tok_h_rd[tt % 2] = tok_tr

        dst = t.R1[:].rearrange("p (a m) -> p a m", a=16)[:, :, tt * 128:(tt + 1) * 128]

        def ev(eng, ps=ps, dst=dst):
            return eng.tensor_copy(out=dst, in_=ps[:].rearrange("p (a m) -> p a m", a=16))
        tok_ev = bld.st("vector", ev, waits=[tok_tr], drain=True, tag=f"A1:ev{tt}")
        tok_ps[tt % 2] = tok_ev
    t.tok_hT = a.tok("ve")


def phase_hB(bld, b):
    t = bld.t
    a = bld.asm
    guard = bld.fence(("pe", "ve", "se")) + [a.tok("dg")]
    tok_ln = load_bc_bf16(bld, t.kb_ln, t.w_ln1_in, t.fp_t1, guard, tag="ln1")
    htm = (t.SP[:, 0:2048], t.SP[:, 2048:4096])
    xb2 = (t.fp_t1, t.fp_t2)
    tok_xb = [None, None]
    tok_htm = [None, None]
    tok_ps = [None, None]
    for tt in range(NT):
        xb = xb2[tt % 2]
        w = guard + [tok_ln] + ([tok_xb[tt % 2]] if tok_xb[tt % 2] else [])
        tok_x = bld.dma("gpsimd", xb, t.x2_scr[b, tt * 128:(tt + 1) * 128, :],
                        waits=w, tag=f"hB:x{tt}")
        m2s = t.fp_m2[:, tt:tt + 1]
        s1s = t.fp_s1[:, tt:tt + 1]
        nb = t.fp_st[:, 30:31]

        def negb(eng, m2s=m2s, s1s=s1s, nb=nb):
            return eng.tensor_scalar(out=nb, in0=m2s, scalar1=s1s, scalar2=-1.0,
                                     op0=ALU.mult, op1=ALU.mult)
        tok_nb = bld.st("vector", negb, waits=[tok_x], drain=True)

        def ap1(eng, xb=xb, s1s=s1s, nb=nb):
            return eng.activation(out=xb, in_=xb, func=AF.Identity, bias=nb, scale=s1s)
        tok_a1 = bld.st("scalar", ap1, waits=[tok_nb, tok_ln], tag=f"hB:a{tt}")
        hb = htm[tt % 2]
        w = [tok_a1] + guard[:1] + ([tok_htm[tt % 2]] if tok_htm[tt % 2] else [])

        def ap2(eng, xb=xb, hb=hb):
            return eng.tensor_mul(out=hb, in0=xb, in1=t.kb_ln)
        tok_h = bld.st("vector", ap2, waits=w, drain=True, tag=f"hB:m{tt}")
        tok_xb[tt % 2] = tok_h
        ps = (t.ps_a, t.ps_b)[tt % 2]
        w = [tok_h] + ([tok_ps[tt % 2]] if tok_ps[tt % 2] else []) + guard[:1]

        def tr(eng, hb=hb, ps=ps):
            ins = None
            for k in range(16):
                ins = eng.matmul(ps[:, k * 128:(k + 1) * 128],
                                 hb[:, k * 128:(k + 1) * 128], t.kb_ident,
                                 start=True, stop=True)
            return ins
        tok_tr = bld.st("tensor", tr, waits=w, tag=f"hB:tr{tt}")
        tok_htm[tt % 2] = tok_tr
        dst = t.R1[:].rearrange("p (a m) -> p a m", a=16)[:, :, tt * 128:(tt + 1) * 128]

        def ev(eng, ps=ps, dst=dst):
            return eng.tensor_copy(out=dst, in_=ps[:].rearrange("p (a m) -> p a m", a=16))
        tok_ev = bld.st("vector", ev, waits=[tok_tr], drain=True, tag=f"hB:ev{tt}")
        tok_ps[tt % 2] = tok_ev
    t.tok_hT = a.tok("ve")


def phase_QKV(bld, b, stage):
    t = bld.t
    a = bld.asm
    wc = t.wc
    guard = bld.fence(("pe", "ve", "se")) + [a.tok("dg"), a.tok("dq")]

    rope_in = t.ropeA_in if stage == "A" else t.ropeB_in
    tok_tab = bld.dma("gpsimd", t.fp_tab.rearrange("p (a m) -> p a m", m=256),
                      rope_in.rearrange("(a p) m -> p a m", p=128),
                      waits=guard, tag=f"{stage}:tab")
    tabv = t.fp_tab.rearrange("p (a m) -> p a m", m=256)

    if stage == "A":
        specs = [("q", "Wq_A", t.bq_Ar_in, 4, True, "hbm", t.qA_scr),
                 ("k", "Wk_A", t.bk_Ar_in, 1, True, "sb_k", None),
                 ("v", "Wv_A", None, 1, False, "sb_v", None)]
    else:
        specs = [("q", "wq_B", t.bq_Br_in, 4, True, "hbm", t.qB_scr),
                 ("k", "wk_B", t.bk_Br_in, 4, True, "hbm", t.kB_scr),
                 ("v", "wv_B", None, 4, False, "hbm", t.vB_scr)]

    wslab = (t.SP[:, 0:8192], t.SP[:, 8192:16384])
    t.kT_A = t.SP[:, 16384:18432]
    t.vtm_A = t.SP[:, 18432:20480]
    ostg8 = [t.SP[:, 16384 + i * 512:16384 + (i + 1) * 512] for i in range(8)]
    ostg2 = [t.SP[:, 20480 + i * 512:20480 + (i + 1) * 512] for i in range(2)]

    slots = [(t.ps_a, i) for i in range(4)] + [(t.ps_b, i) for i in range(4)]
    tok_slot = [None] * 8
    si = [0]
    tok_wslab_rd = [None, None]
    tok_ostg_rd = [None, None]
    tok_ka = [None, None]

    for (nm, wname, bias_in, nch, do_rope, route, out_scr) in specs:
        cw = 512 if nch == 4 else 128
        if stage == "A" and nm == "q":
            ostg = [ostg8[i] for i in range(4)]
        elif stage == "B":
            ostg = ostg8
        else:
            ostg = ostg2
        nbuf = len(ostg)
        tok_ostg_rd = [None] * nbuf
        if stage == "A" and nm == "k":
            kn_init = [False]
        if bias_in is not None:
            bw = bias_in.shape[-1]
            tok_bb = bld.dma("gpsimd", t.fp_bias[:, 0:bw],
                             bass.AP(bias_in.tensor, bias_in.offset, [[0, 128], [1, bw]]),
                             waits=guard + [a.tok("ve")], tag=f"{stage}{nm}:bb")
        else:
            tok_bb = None
        for ch in range(nch):
            par = ch % 2
            w = guard + ([tok_wslab_rd[par]] if tok_wslab_rd[par] else [])
            src32 = _rslab(t.w_in[wname], ch, cw)
            scr16 = _rslab(t.wb[wname], ch, cw)
            tok_w = wc.stream(bld, wslab[par][:, 0:16 * cw], src32, scr16,
                              f"{wname}:{ch}", waits=w)
            for tt in range(NT):
                sl = si[0] % 6
                si[0] += 1
                pst, pso = slots[sl]
                psap = pst[:, pso * 512:pso * 512 + cw]
                w = [tok_w, t.tok_hT] + ([tok_slot[sl]] if tok_slot[sl] else [])

                def mm(eng, wsl=wslab[par], psap=psap, tt=tt, cw=cw):
                    ins = None
                    for kt in range(16):
                        ins = eng.matmul(
                            psap,
                            t.R1[:, kt * 2048 + tt * 128:kt * 2048 + (tt + 1) * 128],
                            wsl[:, kt * cw:(kt + 1) * cw],
                            start=(kt == 0), stop=(kt == 15))
                    return ins
                tok_mm = bld.st("tensor", mm, waits=w, tag=f"{stage}{nm}:mm{ch}.{tt}")
                tok_wslab_rd[par] = tok_mm

                opar = si[0] % nbuf
                ob = ostg[opar][:, 0:cw]
                if do_rope:
                    qb = t.fp_qtmp[si[0] % 2][:, 0:cw]
                    w = [tok_mm, tok_bb]

                    def evb(eng, psap=psap, qb=qb, cw=cw, ch=ch):
                        return eng.tensor_add(out=qb, in0=psap,
                                              in1=t.fp_bias[:, ch * cw:(ch + 1) * cw])
                    tok_ev = bld.st("vector", evb, waits=w,
                                    tag=f"{stage}{nm}:ev{ch}.{tt}")
                    tok_slot[sl] = tok_ev
                    w2 = [tok_tab, tok_ev] + ([tok_ostg_rd[opar]] if tok_ostg_rd[opar] else [])
                    toff = 0 if nm == "q" else 128

                    def rope(eng, qb=qb, ob=ob, tt=tt, cw=cw, toff=toff):
                        nbk = cw // 128
                        rs1 = t.fp_rs[:, 0:256]
                        rs2 = t.fp_rs[:, 256:512]
                        cs = tabv[:, tt, toff + 0:toff + 64]
                        sn = tabv[:, tt, toff + 64:toff + 128]
                        for j in range(nbk):
                            x1 = qb[:, j * 128:j * 128 + 64]
                            x2 = qb[:, j * 128 + 64:(j + 1) * 128]
                            eng.tensor_mul(out=rs1[:, j * 64:(j + 1) * 64], in0=x1, in1=cs)
                            eng.tensor_mul(out=rs2[:, j * 64:(j + 1) * 64], in0=x2, in1=sn)
                        eng.drain()
                        for j in range(nbk):
                            eng.tensor_sub(out=ob[:, j * 128:j * 128 + 64],
                                           in0=rs1[:, j * 64:(j + 1) * 64],
                                           in1=rs2[:, j * 64:(j + 1) * 64])
                        for j in range(nbk):
                            x1 = qb[:, j * 128:j * 128 + 64]
                            x2 = qb[:, j * 128 + 64:(j + 1) * 128]
                            eng.tensor_mul(out=rs1[:, j * 64:(j + 1) * 64], in0=x2, in1=cs)
                            eng.tensor_mul(out=rs2[:, j * 64:(j + 1) * 64], in0=x1, in1=sn)
                        eng.drain()
                        ins = None
                        for j in range(nbk):
                            ins = eng.tensor_add(out=ob[:, j * 128 + 64:(j + 1) * 128],
                                                 in0=rs1[:, j * 64:(j + 1) * 64],
                                                 in1=rs2[:, j * 64:(j + 1) * 64])
                        return ins
                    tok_o = bld.st("vector", rope, waits=w2, drain=True,
                                   tag=f"{stage}{nm}:rp{ch}.{tt}")
                elif route == "hbm":
                    w = [tok_mm] + ([tok_ostg_rd[opar]] if tok_ostg_rd[opar] else [])

                    def evn(eng, psap=psap, ob=ob):
                        return eng.copy(out=ob, in_=psap)
                    tok_o = bld.st("scalar", evn, waits=w, tag=f"{stage}{nm}:ev{ch}.{tt}")
                    tok_slot[sl] = tok_o
                else:
                    def vev(eng, psap=psap, dst=t.vtm_A[:, tt * 128:(tt + 1) * 128]):
                        return eng.copy(out=dst, in_=psap)
                    tok_slot[sl] = bld.st("scalar", vev, waits=[tok_mm], tag=f"Av:ev{tt}")
                    continue

                if stage == "A" and nm == "k":
                    def ksq(eng, ob=ob):
                        return eng.activation(out=t.fp_bias[:, 1024:1152], in_=ob,
                                              func=AF.Square)
                    tok_sq = bld.st("scalar", ksq, waits=[tok_o], tag=f"Ak:sq{tt}")
                    first = not kn_init[0]
                    kn_init[0] = True

                    def krd(eng, first=first):
                        eng.reduce_sum(out=t.fp_sig[:, 120:121],
                                       in_=t.fp_bias[:, 1024:1152], axis=AX)
                        eng.drain()
                        if first:
                            return eng.tensor_copy(out=t.fp_sig[:, 121:122],
                                                   in_=t.fp_sig[:, 120:121])
                        return eng.tensor_tensor(out=t.fp_sig[:, 121:122],
                                                 in0=t.fp_sig[:, 120:121],
                                                 in1=t.fp_sig[:, 121:122], op=ALU.max)
                    tok_kn = bld.st("vector", krd, waits=[tok_sq], drain=True,
                                    tag=f"Ak:kn{tt}")

                if route == "hbm":
                    nheads = cw // 128
                    dst = out_scr[ch * nheads:(ch + 1) * nheads, :, tt:tt + 1, :]
                    dst = dst.rearrange("h p t d -> p h (t d)")
                    tok_d = bld.dma("gpsimd", dst,
                                    ob.rearrange("p (h d) -> p h d", d=128),
                                    waits=[tok_o], tag=f"{stage}{nm}:o{ch}.{tt}")
                    tok_ostg_rd[opar] = tok_d
                else:
                    ksl = 6 + (tt % 2)
                    pst2, pso2 = slots[ksl]
                    ps2 = pst2[:, pso2 * 512:pso2 * 512 + 128]
                    w = [tok_o] + ([tok_ka[tt % 2]] if tok_ka[tt % 2] else [])

                    def ktr(eng, ob=ob, ps2=ps2):
                        return eng.matmul(ps2, ob[:, 0:128], t.kb_ident,
                                          start=True, stop=True)
                    tok_tr = bld.st("tensor", ktr, waits=w, tag=f"Ak:tr{tt}")
                    tok_ostg_rd[opar] = tok_tr

                    def kev(eng, ps2=ps2, dst=t.kT_A[:, tt * 128:(tt + 1) * 128]):
                        return eng.tensor_copy(out=dst, in_=ps2)
                    tok_ka[tt % 2] = bld.st("vector", kev, waits=[tok_tr], drain=True)
        if stage == "A" and nm == "q":
            guard = guard + [a.tok("dg")]
    if stage == "A":
        def kntr(eng):
            return eng.matmul(t.ps_a[0:1, 0:128], t.fp_sig[:, 121:122], t.cb_identf,
                              start=True, stop=True)
        tok_t1 = bld.st("tensor", kntr, waits=[tok_kn, a.tok("se"), a.tok("ve")],
                        tag="Ak:kntr")

        def knmx(eng):
            return eng.reduce_max(out=t.fp_sig[0:1, 122:123], in_=t.ps_a[0:1, 0:128],
                                  axis=AX)
        t.tok_knA = bld.st("vector", knmx, waits=[tok_t1], drain=True, tag="Ak:knmx")
    t.tok_qkv = (a.tok("ve"), a.tok("se"), a.tok("pe"), a.tok("dg"))


def phase_attn(bld, b, stage):
    t = bld.t
    a = bld.asm
    causal = (stage == "B")
    q_scr = t.qA_scr if stage == "A" else t.qB_scr

    ptm = (t.SP[:, 0:2048], t.SP[:, 2048:4096])
    vh_pair = (t.SP[:, 4096:6144], t.SP[:, 6144:8192])
    qtm = t.SP[:, 8192:10240]
    ktm = t.SP[:, 10240:12288]
    qfm_pair = (t.SP[:, 12288:14336], t.SP[:, 14336:16384])
    kfm_pair = (t.SP[:, 16384:18432], t.SP[:, 18432:20480])
    guard = bld.fence(("pe", "ve", "se")) + [bld.asm.tok("dg")]

    tok_qtm_rd = None
    tok_ktm_rd = None
    tok_qfm_rd = [None, None]
    tok_kfm_rd = [None, None]
    tok_v_rd = [None, None]
    tok_ptm_rd = [None, None]
    tok_sig_rd = [None] * 4
    tok_nb_rd = [None, None]
    tok_diag_rd = [None, None]
    tok_psa_rd = None
    tok_trb_ev = None
    tok_pv_ev = None
    last_pv = None

    for h in range(NH):
        par = h % 2
        w = guard + ([tok_qtm_rd] if tok_qtm_rd else [])
        tok_q = bld.dma("gpsimd", qtm, q_scr[h].rearrange("p a m -> p (a m)"),
                        waits=w, tag=f"{stage}at:q{h}")
        wps = [x for x in (tok_trb_ev, tok_pv_ev) if x]

        def qtr(eng, tok=None):
            ins = None
            for k in range(16):
                ins = eng.matmul(t.ps_b[:, k * 128:(k + 1) * 128],
                                 qtm[:, k * 128:(k + 1) * 128], t.kb_ident,
                                 start=True, stop=True)
            return ins
        tok_qt = bld.st("tensor", qtr, waits=[tok_q] + wps, tag=f"{stage}at:qtr{h}")
        tok_qtm_rd = tok_qt
        qh_ = qfm_pair[par]
        w = [tok_qt] + ([tok_qfm_rd[par]] if tok_qfm_rd[par] else [])

        def qev(eng, qh_=qh_):
            return eng.copy(out=qh_, in_=t.ps_b[:])
        tok_qe = bld.st("scalar", qev, waits=w, tag=f"{stage}at:qev{h}")

        def qsq(eng):
            return eng.activation(out=t.fp_t1, in_=qtm, func=AF.Square)
        tok_qs = bld.st("scalar", qsq, waits=[tok_q, a.tok("ve")], tag=f"at:qsq{h}")

        def qrd(eng):
            eng.reduce_sum(out=t.fp_st[:, 32:48],
                           in_=t.fp_t1.rearrange("p (a m) -> p a m", a=16), axis=AX)
            eng.drain()
            return eng.reduce_max(out=t.fp_sig[:, 100:101], in_=t.fp_st[:, 32:48],
                                  axis=AX)
        tok_qr = bld.st("vector", qrd, waits=[tok_qs], drain=True, tag=f"at:qrd{h}")

        if causal:
            w = guard + ([tok_ktm_rd] if tok_ktm_rd else [])
            tok_kd = bld.dma("gpsimd", ktm, t.kB_scr[h].rearrange("p a m -> p (a m)"),
                             waits=w, tag=f"at:k{h}")

            def ksq2(eng):
                return eng.activation(out=t.fp_t1, in_=ktm, func=AF.Square)
            tok_ks = bld.st("scalar", ksq2, waits=[tok_kd, tok_qr], tag=f"at:ksq{h}")

            def krd2(eng):
                eng.reduce_sum(out=t.fp_st[:, 32:48],
                               in_=t.fp_t1.rearrange("p (a m) -> p a m", a=16), axis=AX)
                eng.drain()
                return eng.reduce_max(out=t.fp_sig[:, 101:102], in_=t.fp_st[:, 32:48],
                                      axis=AX)
            tok_kr = bld.st("vector", krd2, waits=[tok_ks], drain=True, tag=f"at:krd{h}")

            def ktr2(eng):
                ins = None
                for k in range(16):
                    ins = eng.matmul(t.ps_b[:, k * 128:(k + 1) * 128],
                                     ktm[:, k * 128:(k + 1) * 128], t.kb_ident,
                                     start=True, stop=True)
                return ins
            tok_kt2 = bld.st("tensor", ktr2, waits=[tok_kd, tok_qe], tag=f"at:ktr{h}")
            tok_ktm_rd = tok_kt2
            kT = kfm_pair[par]
            w = [tok_kt2] + ([tok_kfm_rd[par]] if tok_kfm_rd[par] else [])

            def kev2(eng, kT=kT):
                return eng.copy(out=kT, in_=t.ps_b[:])
            tok_ke = bld.st("scalar", kev2, waits=w, tag=f"at:kev{h}")
            tok_k = tok_ke
            w = guard + ([tok_v_rd[par]] if tok_v_rd[par] else [])
            tok_v = bld.dma("gpsimd", vh_pair[par],
                            t.vB_scr[h].rearrange("p a m -> p (a m)"), waits=w,
                            tag=f"at:v{h}")
            vtm = vh_pair[par]
        else:
            tok_kr = tok_qr
            tok_k = t.tok_qkv[0]
            tok_v = t.tok_qkv[1]
            kT = t.kT_A
            vtm = t.vtm_A
            tok_ke = tok_qe

        def ntr(eng, causal=causal):
            eng.matmul(t.ps_b[0:1, 0:128], t.fp_sig[:, 100:101], t.cb_identf,
                       start=True, stop=True)
            if causal:
                return eng.matmul(t.ps_b[0:1, 512:640], t.fp_sig[:, 101:102],
                                  t.cb_identf, start=True, stop=True)
            return eng.matmul(t.ps_b[0:1, 512:640], t.fp_sig[:, 100:101],
                              t.cb_identf, start=True, stop=True)
        tok_nt = bld.st("tensor", ntr, waits=[tok_qr, tok_kr, tok_ke], tag=f"at:ntr{h}")

        def nmix(eng, causal=causal):
            eng.reduce_max(out=t.fp_sig[0:1, 102:103], in_=t.ps_b[0:1, 0:128], axis=AX)
            if causal:
                eng.reduce_max(out=t.fp_sig[0:1, 103:104], in_=t.ps_b[0:1, 512:640],
                               axis=AX)
            else:
                eng.tensor_copy(out=t.fp_sig[0:1, 103:104], in_=t.fp_sig[0:1, 122:123])
            eng.drain()
            return eng.tensor_mul(out=t.fp_sig[0:1, 106:107],
                                  in0=t.fp_sig[0:1, 102:103],
                                  in1=t.fp_sig[0:1, 103:104])
        tok_nm = bld.st("vector", nmix, waits=[tok_nt] + ([t.tok_knA] if not causal else []),
                        drain=True, tag=f"at:nmix{h}")

        def nsq(eng):
            return eng.activation(out=t.fp_sig[0:1, 107:108], in_=t.fp_sig[0:1, 106:107],
                                  func=AF.Sqrt)
        tok_nq = bld.st("scalar", nsq, waits=[tok_nm], tag=f"at:nsq{h}")

        def nbc(eng):
            return eng.matmul(t.ps_b[:, 1920:1921], t.cb_onesr,
                              t.fp_sig[0:1, 107:108], start=True, stop=True)
        tok_nb1 = bld.st("tensor", nbc, waits=[tok_nq], tag=f"at:nbc{h}")
        nb = t.fp_sig[:, 104 + par:105 + par]
        w = [tok_nb1] + ([tok_nb_rd[par]] if tok_nb_rd[par] else [])

        def nbe(eng, nb=nb):
            return eng.tensor_copy(out=nb, in_=t.ps_b[:, 1920:1921])
        tok_nb = bld.st("vector", nbe, waits=w, drain=True, tag=f"at:nbe{h}")

        tok_q = tok_qe

        tok_s = None
        for half in range(2):
            tok_half_tr = None
            for qi in range(8):
                qt = half * 8 + qi
                nk = (qt + 1) * 128 if causal else 2048
                w = [tok_q, tok_k] + ([tok_psa_rd] if tok_psa_rd else [])

                def smm(eng, qh_=qfm_pair[par], kT=kT, qt=qt, nk=nk):
                    ins = None
                    lhs = qh_[:, qt * 128:(qt + 1) * 128]
                    for c0 in range(0, nk, 512):
                        c1 = min(nk, c0 + 512)
                        ins = eng.matmul(t.ps_a[:, c0:c1], lhs, kT[:, c0:c1],
                                         start=True, stop=True)
                    return ins
                tok_s = bld.st("tensor", smm, waits=w, tag=f"{stage}at:S{h}.{qt}")
                tok_last = tok_s
                if causal:
                    def mka(eng, qt=qt):
                        return eng.tensor_add(
                            out=t.ps_a[:, qt * 128:(qt + 1) * 128],
                            in0=t.ps_a[:, qt * 128:(qt + 1) * 128], in1=t.cb_cmask)
                    tok_last = bld.st("vector", mka, waits=[tok_s, t.tok_consts[1]],
                                      drain=True, tag=f"at:mk{qt}")
                sl = qt % 4
                sg = t.fp_sig[:, sl * 4 + 1:sl * 4 + 2]
                rc = t.fp_sig[:, sl * 4 + 2:sl * 4 + 3]
                pb = ptm[qt % 2]
                w = [tok_last, tok_nb] + ([tok_ptm_rd[qt % 2]] if tok_ptm_rd[qt % 2] else [])
                w += [tok_sig_rd[sl]] if tok_sig_rd[sl] else []

                def ex(eng, pb=pb, nb=nb, sg=sg, nk=nk):
                    return eng.activation(out=pb[:, 0:nk], in_=t.ps_a[:, 0:nk],
                                          func=AF.Exp, bias=nb, scale=1.0,
                                          accum_out=sg)
                tok_ex = bld.st("scalar", ex, waits=w, tag=f"at:ex{qt}")
                tok_psa_rd = tok_ex
                dgb = t.kb_diag[qt % 2]
                w = [tok_ex] + ([tok_diag_rd[qt % 2]] if tok_diag_rd[qt % 2] else [])

                def rcd(eng, sg=sg, rc=rc, dgb=dgb):
                    eng.reciprocal(out=rc, in_=sg)
                    eng.drain()
                    return eng.tensor_scalar(out=dgb, in0=t.kb_ident, scalar1=rc,
                                             scalar2=None, op0=ALU.mult)
                tok_rcd = bld.st("vector", rcd, waits=w, drain=True, tag=f"at:rc{qt}")
                tok_sig_rd[sl] = tok_rcd

                nkt = (qt + 1) if causal else 16
                tok_t = None
                for g0 in range(0, nkt, 8):
                    g1 = min(nkt, g0 + 8)
                    w = [tok_rcd] + ([tok_trb_ev] if tok_trb_ev else [])

                    def trm(eng, pb=pb, dgb=dgb, g0=g0, g1=g1):
                        ins = None
                        for k in range(g0, g1):
                            ins = eng.matmul(
                                t.ps_b[:, (k - g0) * 128:(k - g0 + 1) * 128],
                                pb[:, k * 128:(k + 1) * 128], dgb,
                                start=True, stop=True)
                        return ins
                    tok_t = bld.st("tensor", trm, waits=w, tag=f"at:tr{qt}.{g0}")
                    dview = t.R1[:, 0:16384].rearrange("p (a m) -> p a m", a=16)[
                        :, g0:g1, qi * 128:(qi + 1) * 128]

                    def evt(eng, dview=dview, g0=g0, g1=g1):
                        return eng.copy(
                            out=dview,
                            in_=t.ps_b[:, 0:(g1 - g0) * 128].rearrange(
                                "p (a m) -> p a m", a=(g1 - g0)))
                    tok_trb_ev = bld.st("scalar", evt, waits=[tok_t], tag=f"at:evt{qt}.{g0}")
                tok_ptm_rd[qt % 2] = tok_t
                tok_diag_rd[qt % 2] = tok_t
                tok_half_tr = tok_trb_ev

            nkt_h = 8 * (half + 1) if causal else 16
            w = [tok_half_tr, tok_v] + ([tok_pv_ev] if tok_pv_ev else [])

            def pv(eng, vtm=vtm, half=half, nkt_h=nkt_h, causal=causal):
                ins = None
                for kt in range(nkt_h):
                    q0 = kt * 128 - half * 1024 if causal else 0
                    for c0 in (0, 512):
                        c0a = max(c0, q0)
                        c1 = c0 + 512
                        if c0a >= c1:
                            continue
                        if causal:
                            lk = min(nkt_h - 1, (half * 1024 + c1 - 1) // 128)
                        else:
                            lk = nkt_h - 1
                        ins = eng.matmul(t.ps_b[:, 1024 + c0a:1024 + c1],
                                         vtm[:, kt * 128:(kt + 1) * 128],
                                         t.R1[:, kt * 1024 + c0a: kt * 1024 + c1],
                                         start=(kt == 0), stop=(kt == lk))
                return ins
            tok_pv = bld.st("tensor", pv, waits=w, tag=f"{stage}at:pv{h}.{half}")
            last_pv = tok_pv
            if h < 8:
                cdst = t.R2[:, h * 2048 + half * 1024: h * 2048 + (half + 1) * 1024]
            else:
                cdst = t.R1[:, 16384 + (h - 8) * 2048 + half * 1024:
                            16384 + (h - 8) * 2048 + (half + 1) * 1024]

            def evc(eng, cdst=cdst):
                return eng.tensor_copy(out=cdst, in_=t.ps_b[:, 1024:2048])
            tok_pv_ev = bld.st("vector", evc, waits=[tok_pv], drain=True,
                               tag=f"{stage}at:evc{h}.{half}")
        tok_qfm_rd[par] = tok_s
        tok_nb_rd[par] = tok_ex
        if causal:
            tok_kfm_rd[par] = tok_s
            tok_v_rd[par] = last_pv
    t.tok_attn = (a.tok("ve"), a.tok("se"), a.tok("pe"))


def phase_wo(bld, b, stage):
    t = bld.t
    a = bld.asm
    wc = t.wc
    wname = "Wo_A" if stage == "A" else "wo_B"
    guard = [t.tok_attn[2], t.tok_attn[0], t.tok_attn[1]]

    slab_toks = []
    for ct in range(16):
        dst = (t.R1[:, ct * 2048:(ct + 1) * 2048] if ct < 8
               else t.SP[:, (ct - 8) * 2048:(ct - 7) * 2048])
        src = t.w_in[wname].rearrange("(a p) m -> p a m", p=128)[:, ct:ct + 1, :]
        scrp = t.wb[wname].rearrange("(a p) m -> p a m", p=128)[:, ct:ct + 1, :]
        tok = wc.stream(bld, dst, src, scrp, f"{wname}:{ct}", waits=guard)
        slab_toks.append(tok)

    def wo_slab(ct, c0, c1):
        if ct < 8:
            return t.R1[:, ct * 2048 + c0: ct * 2048 + c1]
        return t.SP[:, (ct - 8) * 2048 + c0:(ct - 8) * 2048 + c1]

    def ctx_ap(ct, tt):
        half = tt // 8
        q0 = (tt % 8) * 128
        if ct < 8:
            base = ct * 2048 + half * 1024 + q0
            return t.R2[:, base:base + 128]
        base = 16384 + (ct - 8) * 2048 + half * 1024 + q0
        return t.R1[:, base:base + 128]

    boe_bc = t.kb_boe
    bv_col = t.cb_bvA if stage == "A" else t.cb_bvB
    if b == 0:
        w = [t.tok_consts[2 if stage == "A" else 3]] + guard

        def bvcv(eng):
            return eng.copy(out=t.kb_bvtmp, in_=bv_col)
        tok_bv = bld.st("scalar", bvcv, waits=w, tag=f"{stage}:bvcv")
        w = list(slab_toks) + [tok_bv]

        def boe(eng):
            ins = None
            for ct in range(16):
                for c0 in range(0, 2048, 512):
                    ins = eng.matmul(t.ps_a[0:1, c0:c0 + 512],
                                     t.kb_bvtmp[:, ct:ct + 1],
                                     wo_slab(ct, c0, c0 + 512),
                                     start=(ct == 0), stop=(ct == 15))
            return ins
        tok_b1 = bld.st("tensor", boe, waits=w, tag=f"{stage}:boeff")
        if stage == "A":
            tok_bo = bld.dma("gpsimd", t.fp_borow, t.bo_A_in, waits=guard, tag="c:boA")

            def boadd(eng):
                return eng.tensor_add(out=t.fp_borow, in0=t.ps_a[0:1, 0:2048],
                                      in1=t.fp_borow)
            tok_b2 = bld.st("vector", boadd, waits=[tok_b1, tok_bo], drain=True)
        else:
            def bocp(eng):
                return eng.tensor_copy(out=t.fp_borow, in_=t.ps_a[0:1, 0:2048])
            tok_b2 = bld.st("vector", bocp, waits=[tok_b1], drain=True)
        idx = 0 if stage == "A" else 1
        tok_b3 = bld.dma("gpsimd", t.boe_scr[idx:idx + 1, :], t.fp_borow, waits=[tok_b2])
        tok_b4 = bld.dma("gpsimd", t.fp_t1,
                         bass.AP(t.boe_scr.tensor, idx * D, [[0, 128], [1, D]]),
                         waits=[tok_b3])

        def bcv(eng):
            return eng.copy(out=boe_bc, in_=t.fp_t1)
        tok_boe = bld.st("scalar", bcv, waits=[tok_b4], tag=f"{stage}:boebc")
    else:
        idx = 0 if stage == "A" else 1
        tok_boe = load_bc_bf16(bld, boe_bc, t.boe_scr[idx:idx + 1, :], t.fp_t1,
                               guard, tag=f"{stage}boe")

    xb_pair = (t.fp_t2, t.fp_t3)
    ab = t.fp_t1
    tok_ps = [None, None]
    tok_ab_rd = None
    tok_xb_rd = [None, None]
    dg_guard = a.tok("dg")

    for tt in range(NT):
        ps = (t.ps_a, t.ps_b)[tt % 2]
        w = list(slab_toks) + [t.tok_attn[0]] + ([tok_ps[tt % 2]] if tok_ps[tt % 2] else [])
        if b == 0 and tt == 0:
            w.append(tok_boe)

        def wmm(eng, tt=tt, ps=ps):
            ins = None
            for ct in range(16):
                for oc in range(4):
                    ins = eng.matmul(ps[:, oc * 512:(oc + 1) * 512],
                                     ctx_ap(ct, tt),
                                     wo_slab(ct, oc * 512, (oc + 1) * 512),
                                     start=(ct == 0), stop=(ct == 15))
            return ins
        tok_mm = bld.st("tensor", wmm, waits=w, tag=f"{stage}wo:mm{tt}")

        xb = xb_pair[tt % 2]
        w = [dg_guard] + guard + ([tok_xb_rd[tt % 2]] if tok_xb_rd[tt % 2] else [])
        src = (t.x[b, tt * 128:(tt + 1) * 128, :] if stage == "A"
               else t.x2_scr[b, tt * 128:(tt + 1) * 128, :])
        tok_x = bld.dma("gpsimd", xb, src, waits=w, tag=f"{stage}wo:x{tt}")

        if stage == "A":
            w = [tok_mm, tok_boe] + ([tok_ab_rd] if tok_ab_rd else [])

            def ev(eng, ps=ps):
                return eng.tensor_add(out=ab, in0=ps[:], in1=boe_bc)
            tok_ev = bld.st("vector", ev, waits=w, drain=True, tag=f"Awo:ev{tt}")
        else:
            m2s = t.fp_m2[:, tt:tt + 1]
            r2s = t.fp_r2[:, tt:tt + 1]
            nb = t.fp_st[:, 28:29]

            def negb(eng, m2s=m2s, r2s=r2s, nb=nb):
                return eng.tensor_scalar(out=nb, in0=m2s, scalar1=r2s, scalar2=-1.0,
                                         op0=ALU.mult, op1=ALU.mult)
            tok_nb = bld.st("vector", negb, waits=[tok_x], drain=True)

            def zap(eng, xb=xb, r2s=r2s, nb=nb):
                return eng.activation(out=xb, in_=xb, func=AF.Identity, bias=nb, scale=r2s)
            tok_z = bld.st("scalar", zap, waits=[tok_nb], tag=f"Bwo:z{tt}")
            w = [tok_z, tok_mm, tok_boe] + ([tok_ab_rd] if tok_ab_rd else [])

            def ev(eng, ps=ps, xb=xb):
                eng.tensor_add(out=ab, in0=ps[:], in1=xb)
                eng.drain()
                return eng.tensor_add(out=ab, in0=ab, in1=boe_bc)
            tok_ev = bld.st("vector", ev, waits=w, drain=True, tag=f"Bwo:ev{tt}")
            tok_xin = bld.dma("gpsimd", t.xin_scr[tt * 128:(tt + 1) * 128, :], ab,
                              waits=[tok_ev], tag=f"Bwo:xin{tt}")
        tok_ps[tt % 2] = tok_ev

        tok_bn = ln_stats(bld, ab, 2048, [tok_ev], tag=f"{stage}wo.{tt}")

        if stage == "A":
            tok_sc, rstd, nmr = ln_scale(bld, tok_bn, EPS_LN, tag=f"Awo.{tt}")
            w = [tok_sc, tok_x]

            def x2c(eng, xb=xb, rstd=rstd, nmr=nmr):
                eng.tensor_scalar(out=xb, in0=xb, scalar1=nmr, scalar2=None,
                                  op0=ALU.add)
                eng.drain()
                return eng.scalar_tensor_tensor(out=xb, in0=ab, scalar=rstd, in1=xb,
                                                op0=ALU.mult, op1=ALU.add)
            tok_x2 = bld.st("vector", x2c, waits=w, drain=True, tag=f"Awo:x2.{tt}")
            tok_ab_rd = tok_x2
            tok_bn2 = ln_stats(bld, xb, 2048, [tok_x2], tag=f"Ax2.{tt}")
            mv = t.fp_st[:, 24:26]
            r2d = t.fp_r2[:, tt:tt + 1]
            m2d = t.fp_m2[:, tt:tt + 1]

            def sq2(eng, mv=mv, r2d=r2d):
                return eng.activation(out=r2d, in_=mv[:, 1:2], func=AF.Sqrt, bias=EPS_LN)
            tok_sq2 = bld.st("scalar", sq2, waits=[tok_bn2])

            std = t.fp_st[:, 29:30]
            s1d = t.fp_s1[:, tt:tt + 1]

            def sq3(eng, mv=mv, std=std):
                return eng.activation(out=std, in_=mv[:, 1:2], func=AF.Sqrt,
                                      bias=EPS_LN * EPS_RMS, scale=1.0 + EPS_RMS)
            tok_sq3 = bld.st("scalar", sq3, waits=[tok_bn2])

            def fin2(eng, mv=mv, r2d=r2d, m2d=m2d, std=std, s1d=s1d):
                eng.reciprocal(out=r2d, in_=r2d)
                eng.reciprocal(out=s1d, in_=std)
                eng.drain()
                return eng.tensor_copy(out=m2d, in_=mv[:, 0:1])
            tok_f2 = bld.st("vector", fin2, waits=[tok_sq2, tok_sq3], drain=True)
            tok_w = bld.dma("gpsimd", t.x2_scr[b, tt * 128:(tt + 1) * 128, :], xb,
                            waits=[tok_x2, tok_f2], tag=f"Awo:x2w{tt}")
            tok_xb_rd[tt % 2] = tok_w
            if _OUT_STAGE == "x2":
                bld.dma("gpsimd", t.out[b, tt * 128:(tt + 1) * 128, :], xb,
                        waits=[tok_x2, tok_f2])
        else:
            mv = t.fp_st[:, 24:26]
            rstd = t.fp_st[:, 26:27]
            rrd = t.fp_rr[:, tt:tt + 1]

            def msq(eng, mv=mv, rstd=rstd):
                return eng.scalar_tensor_tensor(out=rstd, in0=mv[:, 0:1],
                                                scalar=mv[:, 0:1], in1=mv[:, 1:2],
                                                op0=ALU.mult, op1=ALU.add)
            tok_m = bld.st("vector", msq, waits=[tok_bn], drain=True)

            def sq(eng, rstd=rstd):
                return eng.activation(out=rstd, in_=rstd, func=AF.Sqrt, bias=EPS_RMS)
            tok_sq = bld.st("scalar", sq, waits=[tok_m], tag=f"Bwo:rms{tt}")

            def rcp(eng, rstd=rstd, rrd=rrd):
                eng.reciprocal(out=rrd, in_=rstd)
                eng.drain()
                return eng.engine_nop()
            tok_r = bld.st("vector", rcp, waits=[tok_sq], drain=True)
            tok_ab_rd = tok_xin
            tok_xb_rd[tt % 2] = tok_ev
    t.tok_wo = (a.tok("ve"), a.tok("se"), a.tok("pe"), a.tok("dg"))


def phase_router(bld, b):
    t = bld.t
    a = bld.asm
    wc = t.wc
    guard = [t.tok_wo[2], t.tok_wo[0], t.tok_wo[1]]
    dg_guard = t.tok_wo[3]

    htm = (t.SP[:, 0:2048], t.SP[:, 2048:4096])
    xb2 = (t.fp_t1, t.fp_t2)
    tok_ln = load_bc_bf16(bld, t.kb_ln, t.w_ln2_in, t.fp_t1,
                          guard + [dg_guard], tag="ln2")
    tok_ps = [None, None]
    tok_htm = [None, None]
    tok_xb = [None, None]
    for tt in range(NT):
        xb = xb2[tt % 2]
        w = [dg_guard, tok_ln] + guard + ([tok_xb[tt % 2]] if tok_xb[tt % 2] else [])
        tok_x = bld.dma("gpsimd", xb, t.xin_scr[tt * 128:(tt + 1) * 128, :], waits=w,
                        tag=f"hp:x{tt}")
        rr = t.fp_rr[:, tt:tt + 1]
        hb = htm[tt % 2]
        w = [tok_x] + guard + ([tok_htm[tt % 2]] if tok_htm[tt % 2] else [])

        def ap1(eng, xb=xb, rr=rr):
            return eng.activation(out=xb, in_=xb, func=AF.Identity, scale=rr)
        tok_a1 = bld.st("scalar", ap1, waits=w, tag=f"hp:a{tt}")

        def ap2(eng, xb=xb, hb=hb):
            return eng.tensor_mul(out=hb, in0=xb, in1=t.kb_ln)
        tok_h = bld.st("vector", ap2, waits=[tok_a1, tok_ln], drain=True,
                       tag=f"hp:m{tt}")
        tok_xb[tt % 2] = tok_h
        ps = (t.ps_a, t.ps_b)[tt % 2]
        w = [tok_h] + ([tok_ps[tt % 2]] if tok_ps[tt % 2] else []) + guard[:1]

        def tr(eng, hb=hb, ps=ps):
            ins = None
            for k in range(16):
                ins = eng.matmul(ps[:, k * 128:(k + 1) * 128],
                                 hb[:, k * 128:(k + 1) * 128], t.kb_ident,
                                 start=True, stop=True)
            return ins
        tok_tr = bld.st("tensor", tr, waits=w, tag=f"hp:tr{tt}")
        tok_htm[tt % 2] = tok_tr
        dst = t.R1[:].rearrange("p (a m) -> p a m", a=16)[:, :, tt * 128:(tt + 1) * 128]

        def ev(eng, ps=ps, dst=dst):
            return eng.tensor_copy(out=dst, in_=ps[:].rearrange("p (a m) -> p a m", a=16))
        tok_ev = bld.st("vector", ev, waits=[tok_tr], drain=True, tag=f"hp:ev{tt}")
        tok_ps[tt % 2] = tok_ev
    t.tok_hpT = a.tok("ve")

    slab = t.SP[:, 4096:4240]
    key = "wre"
    if key not in wc.seen:
        wc.seen.add(key)
        fb = t.fp_conv[0]
        w = [wc.fb_tok[0]] if wc.fb_tok[0] else []
        tok_l1 = bld.dma("sync", fb[:, 0:128].rearrange("p (a m) -> p a m", m=8),
                         _r16(t.w_in["w_router"]), waits=w, tag="wre:l1")
        tok_l2 = bld.dma("sync", fb[:, 128:144].rearrange("p (a m) -> p a m", m=1),
                         _r16(t.w_in["w_sh_eg"]), tag="wre:l2", slow=True)

        def cv(eng, fb=fb, slab=slab):
            eng.copy(out=slab.rearrange("p (a m) -> p a m", m=9)[:, :, 0:8],
                     in_=fb[:, 0:128].rearrange("p (a m) -> p a m", m=8))
            return eng.copy(out=slab.rearrange("p (a m) -> p a m", m=9)[:, :, 8:9],
                            in_=fb[:, 128:144].rearrange("p (a m) -> p a m", m=1))
        tok_cv = bld.st("scalar", cv, waits=[tok_l1, tok_l2], tag="wre:cv")
        wc.fb_tok[0] = tok_cv
        bld.dma("gpsimd", _r16(t.wb["wre"]),
                slab.rearrange("p (a m) -> p a m", m=9), waits=[tok_cv])
        tok_wre = tok_cv
    else:
        tok_wre = bld.dma("sync", slab.rearrange("p (a m) -> p a m", m=9),
                          _r16(t.wb["wre"]), waits=guard, tag="wre:ld")

    lg = t.fp_st[:, 0:16]
    tok_ps2 = [None, None]
    tok_lg = None
    for tt in range(NT):
        ps = (t.ps_a, t.ps_b)[tt % 2]
        w = [t.tok_hpT, tok_wre] + ([tok_ps2[tt % 2]] if tok_ps2[tt % 2] else [])

        def rmm(eng, tt=tt, ps=ps):
            ins = None
            for kt in range(16):
                ins = eng.matmul(ps[:, 0:9],
                                 t.R1[:, kt * 2048 + tt * 128: kt * 2048 + (tt + 1) * 128],
                                 slab[:, kt * 9:(kt + 1) * 9],
                                 start=(kt == 0), stop=(kt == 15))
            return ins
        tok_mm = bld.st("tensor", rmm, waits=w, tag=f"rt:mm{tt}")
        w = [tok_mm] + ([tok_lg] if tok_lg else [])

        def sfm(eng, ps=ps):
            return eng.reduce_max(out=lg[:, 8:9], in_=ps[:, 0:8], axis=AX, negate=True)
        tok_m = bld.st("vector", sfm, waits=w, drain=True)

        def exf(eng, ps=ps):
            return eng.activation(out=lg[:, 0:8], in_=ps[:, 0:8], func=AF.Exp,
                                  bias=lg[:, 8:9], scale=1.0, accum_out=lg[:, 9:10])
        tok_e = bld.st("scalar", exf, waits=[tok_m], tag=f"rt:ex{tt}")

        def sgm(eng, ps=ps):
            return eng.activation(out=lg[:, 10:11], in_=ps[:, 8:9], func=AF.Sigmoid)
        tok_g = bld.st("scalar", sgm, waits=[tok_e], tag=f"rt:sg{tt}")
        tok_ps2[tt % 2] = tok_g
        cdst = t.c_all[:, tt * 9:(tt + 1) * 9]

        def top2(eng, cdst=cdst):
            eng.reciprocal(out=lg[:, 11:12], in_=lg[:, 9:10])
            eng.drain()
            eng.tensor_scalar(out=lg[:, 0:8], in0=lg[:, 0:8], scalar1=lg[:, 11:12],
                              scalar2=None, op0=ALU.mult)
            eng.drain()
            eng.reduce_max(out=lg[:, 8:9], in_=lg[:, 0:8], axis=AX)
            eng.drain()
            eng.tensor_scalar(out=cdst[:, 0:8], in0=lg[:, 0:8], scalar1=lg[:, 8:9],
                              scalar2=None, op0=ALU.is_ge)
            eng.drain()
            eng.scalar_tensor_tensor(out=lg[:, 0:8], in0=cdst[:, 0:8], scalar=-2.0,
                                     in1=lg[:, 0:8], op0=ALU.mult, op1=ALU.add)
            eng.drain()
            eng.reduce_max(out=lg[:, 8:9], in_=lg[:, 0:8], axis=AX)
            eng.drain()
            eng.scalar_tensor_tensor(out=lg[:, 0:8], in0=cdst[:, 0:8], scalar=2.0,
                                     in1=lg[:, 0:8], op0=ALU.mult, op1=ALU.add)
            eng.drain()
            eng.tensor_scalar(out=cdst[:, 0:8], in0=lg[:, 0:8], scalar1=lg[:, 8:9],
                              scalar2=None, op0=ALU.is_ge)
            eng.drain()
            eng.tensor_mul(out=cdst[:, 0:8], in0=cdst[:, 0:8], in1=lg[:, 0:8])
            eng.drain()
            return eng.tensor_copy(out=cdst[:, 8:9], in_=lg[:, 10:11])
        tok_lg = bld.st("vector", top2, waits=[tok_g], drain=True, tag=f"rt:t2.{tt}")
    t.tok_router = a.tok("ve")


def phase_moe(bld, b):
    t = bld.t
    a = bld.asm
    wc = t.wc
    acc = t.fp_acc
    wg = (t.SP[:, 0:2048], t.SP[:, 2048:4096])
    wd_pair = (t.SP[:, 4096:9728], t.SP[:, 9728:15360])
    wu = (t.SP[:, 15360:17408], t.SP[:, 17408:19456])
    sGb = t.R2[:, 11264:11776]
    uCb = t.R2[:, 11776:12288]
    aT = (t.R2[:, 0:5632], t.R2[:, 5632:11264])

    guard = [a.tok("pe"), t.tok_router, a.tok("se")]
    tok_wg = [None, None]
    tok_wu = [None, None]
    tok_wd = [None, None]
    tok_aT_rd = [None, None]
    tok_g_ps = [None, None]
    tok_u_ps = [None, None]
    tok_d_ps = [None] * 4
    tok_acc = [None] * 4
    tok_sG = None
    tok_uC = None
    tok_am = None
    tok_fin_x = None

    def gu_block(e_or_sec, it, kt_base_key, wgsrc, wgscr, wusrc, wuscr, ch_lo, apar, tag):
        nonlocal tok_sG, tok_uC, tok_am
        wpar = it % 2
        w = guard + ([tok_wg[wpar]] if tok_wg[wpar] else [])
        tkg = wc.stream(bld, wg[wpar], wgsrc, wgscr, f"wg{kt_base_key}", waits=w)
        w = guard + ([tok_wu[wpar]] if tok_wu[wpar] else [])
        tku = wc.stream(bld, wu[wpar], wusrc, wuscr, f"wu{kt_base_key}", waits=w)
        gps = t.ps_a[:, (it % 2) * 512:(it % 2) * 512 + 512]
        ups = t.ps_a[:, (2 + it % 2) * 512:(2 + it % 2) * 512 + 512]
        w = [tkg, t.tok_hpT] + ([tok_g_ps[it % 2]] if tok_g_ps[it % 2] else [])

        def gmm(eng, wg_=wg[wpar], gps=gps, ch_lo=ch_lo):
            ins = None
            for kt in range(16):
                ins = eng.matmul(gps, wg_[:, kt * 128:(kt + 1) * 128],
                                 t.R1[:, kt * 2048 + ch_lo: kt * 2048 + ch_lo + TC],
                                 start=(kt == 0), stop=(kt == 15))
            return ins
        tok_g = bld.st("tensor", gmm, waits=w, tag=f"{tag}:g")
        tok_wg[wpar] = tok_g
        w = [tku, t.tok_hpT] + ([tok_u_ps[it % 2]] if tok_u_ps[it % 2] else [])

        def umm(eng, wu_=wu[wpar], ups=ups, ch_lo=ch_lo):
            ins = None
            for kt in range(16):
                ins = eng.matmul(ups, wu_[:, kt * 128:(kt + 1) * 128],
                                 t.R1[:, kt * 2048 + ch_lo: kt * 2048 + ch_lo + TC],
                                 start=(kt == 0), stop=(kt == 15))
            return ins
        tok_u = bld.st("tensor", umm, waits=w, tag=f"{tag}:u")
        tok_wu[wpar] = tok_u
        w = [tok_g] + ([tok_sG] if tok_sG else [])

        def esg(eng, gps=gps):
            return eng.activation(out=sGb, in_=gps, func=AF.Silu)
        tok_sg = bld.st("scalar", esg, waits=w, tag=f"{tag}:sg")
        tok_g_ps[it % 2] = tok_sg
        w = [tok_u] + ([tok_uC] if tok_uC else [])

        def euc(eng, ups=ups):
            return eng.copy(out=uCb, in_=ups)
        tok_uc = bld.st("scalar", euc, waits=w, tag=f"{tag}:uc")
        tok_u_ps[it % 2] = tok_uc
        adst = aT[apar][:, it * 512:(it + 1) * 512]
        w = [tok_sg, tok_uc]
        if it == 0 and tok_aT_rd[apar]:
            w.append(tok_aT_rd[apar])
        if tok_am:
            w.append(tok_am)

        def amul(eng, adst=adst):
            return eng.tensor_mul(out=adst, in0=sGb, in1=uCb)
        tok_am = bld.st("vector", amul, waits=w, drain=True, tag=f"{tag}:am")
        tok_sG = tok_am
        tok_uC = tok_am
        return tok_am

    def down_block(oc, wdsrc, wdscr, key, tok_a_done, ch, cidx, tag):
        wpar = oc % 2
        w = guard + ([tok_wd[wpar]] if tok_wd[wpar] else [])
        tkd = wc.stream(bld, wd_pair[wpar], wdsrc, wdscr, key, waits=w)
        last = None
        for tt in range(4):
            dps = t.ps_b[:, tt * 512:(tt + 1) * 512]
            w = [tkd, tok_a_done] + ([tok_d_ps[tt]] if tok_d_ps[tt] else [])
            apar = cidx[0]

            def dmm(eng, wd_=wd_pair[wpar], dps=dps, tt=tt, apar=apar):
                ins = None
                for it in range(NIT):
                    ins = eng.matmul(dps,
                                     aT[apar][:, it * 512 + tt * 128: it * 512 + (tt + 1) * 128],
                                     wd_[:, it * 512:(it + 1) * 512],
                                     start=(it == 0), stop=(it == NIT - 1))
                return ins
            tok_d = bld.st("tensor", dmm, waits=w, tag=f"{tag}:d{tt}")
            tok_wd[wpar] = tok_d
            last = tok_d
            cap = t.c_all[:, (ch * 4 + tt) * 9 + cidx[1]:(ch * 4 + tt) * 9 + cidx[1] + 1]
            accs = acc[:, tt * 2048 + oc * 512: tt * 2048 + (oc + 1) * 512]
            w = [tok_d, t.tok_router] + ([tok_acc[tt]] if tok_acc[tt] else [])
            if cidx[2]:
                def ac(eng, dps=dps, cap=cap, accs=accs):
                    return eng.tensor_scalar(out=accs, in0=dps, scalar1=cap,
                                             scalar2=None, op0=ALU.mult)
            else:
                def ac(eng, dps=dps, cap=cap, accs=accs):
                    return eng.scalar_tensor_tensor(out=accs, in0=dps, scalar=cap,
                                                    in1=accs, op0=ALU.mult, op1=ALU.add)
            tok_a2 = bld.st("vector", ac, waits=w, drain=True, tag=f"{tag}:ac{tt}")
            tok_d_ps[tt] = tok_a2
            tok_acc[tt] = tok_a2
        return last

    for ch in range(NCH):
        ch_lo = ch * TC
        for e in range(N_EXP):
            apar = e % 2
            tok_a_done = None
            for it in range(NIT):
                tok_a_done = gu_block(
                    e, it, f"G:{e}:{it}",
                    _rslab(t.w_in["w_gate_e"][e], it), _rslab(t.wb["w_gate_e"][e], it),
                    _rslab(t.w_in["w_up_e"][e], it), _rslab(t.wb["w_up_e"][e], it),
                    ch_lo, apar, f"m{e}.{it}")
            tok_dlast = None
            for oc in range(4):
                tok_dlast = down_block(
                    oc, _rslab(t.w_in["w_down_e"][e], oc, 512),
                    _rslab(t.wb["w_down_e"][e], oc, 512),
                    f"wd:{e}:{oc}", tok_a_done, ch, (apar, e, e == 0),
                    f"m{e}")
            tok_aT_rd[apar] = tok_dlast
        for sec in range(4):
            apar = sec % 2
            tok_a_done = None
            for it in range(NIT):
                git = sec * NIT + it
                tok_a_done = gu_block(
                    sec, it, f"S:{git}",
                    _rslab(t.w_in["w_sh_gate"], git), _rslab(t.wb["w_sh_gate"], git),
                    _rslab(t.w_in["w_sh_up"], git), _rslab(t.wb["w_sh_up"], git),
                    ch_lo, apar, f"s{sec}.{it}")
            base = sec * NIT * 128
            tok_dlast = None
            for oc in range(4):
                tok_dlast = down_block(
                    oc, _rslab(t.w_in["w_sh_down"][base:base + NIT * 128, :], oc, 512),
                    _rslab(t.wb["w_sh_down"][base:base + NIT * 128, :], oc, 512),
                    f"wsd:{sec}:{oc}", tok_a_done, ch, (apar, 8, False), f"s{sec}")
            tok_aT_rd[apar] = tok_dlast

        for tt in range(4):
            gt = ch * 4 + tt
            xi = t.fp_t1
            w = [tok_acc[tt]] + ([tok_fin_x] if tok_fin_x else [])
            tok_xi = bld.dma("gpsimd", xi, t.xin_scr[gt * 128:(gt + 1) * 128, :],
                             waits=w, tag=f"fin:xi{gt}")
            accs = acc[:, tt * 2048:(tt + 1) * 2048]

            def vadd(eng, accs=accs, xi=xi):
                return eng.tensor_add(out=accs, in0=accs, in1=xi)
            tok_v = bld.st("vector", vadd, waits=[tok_xi, tok_acc[tt]], drain=True,
                           tag=f"fin:v{gt}")
            tok_bn = ln_stats(bld, accs, 2048, [tok_v], tag=f"fin.{gt}")
            tok_sc, rstd, nmr = ln_scale(bld, tok_bn, EPS_LN, tag=f"fin.{gt}")
            tok_x2 = bld.dma("gpsimd", xi, t.x2_scr[b, gt * 128:(gt + 1) * 128, :],
                             waits=[tok_v], tag=f"fin:x2{gt}")

            def fino(eng, accs=accs, xi=xi, rstd=rstd, nmr=nmr):
                eng.tensor_scalar(out=xi, in0=xi, scalar1=nmr, scalar2=None,
                                  op0=ALU.add)
                eng.drain()
                return eng.scalar_tensor_tensor(out=xi, in0=accs, scalar=rstd, in1=xi,
                                                op0=ALU.mult, op1=ALU.add)
            tok_f = bld.st("vector", fino, waits=[tok_sc, tok_x2], drain=True,
                           tag=f"fin:o{gt}")
            tok_fin_x = bld.dma("gpsimd", t.out[b, gt * 128:(gt + 1) * 128, :], xi,
                                waits=[tok_f], tag=f"fin:out{gt}")
            tok_acc[tt] = tok_f


def kernel(**inputs):
    xs = np.ascontiguousarray(inputs["L0_attn_x"])
    consts = _consts()
    base = {
        "Wq_A": inputs["L0_attn_Wq"], "Wk_A": inputs["L0_attn_Wk"],
        "Wv_A": inputs["L0_attn_Wv"], "Wo_A": inputs["L0_attn_Wo"],
        "wq_B": inputs["L0_ffn_wq"], "wk_B": inputs["L0_ffn_wk"],
        "wv_B": inputs["L0_ffn_wv"], "wo_B": inputs["L0_ffn_wo"],
        "w_router": inputs["L0_ffn_w_router"],
        "w_gate_e": inputs["L0_ffn_w_gate_e"], "w_up_e": inputs["L0_ffn_w_up_e"],
        "w_down_e": inputs["L0_ffn_w_down_e"],
        "w_sh_gate": inputs["L0_ffn_w_sh_gate"], "w_sh_up": inputs["L0_ffn_w_sh_up"],
        "w_sh_down": inputs["L0_ffn_w_sh_down"], "w_sh_eg": inputs["L0_ffn_w_sh_eg"],
        "bq_Ar": inputs["L0_attn_bq"].reshape(1, 2048),
        "bk_Ar": inputs["L0_attn_bk"].reshape(1, 128),
        "bv_Acol": np.tile(inputs["L0_attn_bv"], 16).reshape(16, 128).T,
        "bo_A": inputs["L0_attn_bo"].reshape(1, 2048),
        "bq_Br": inputs["L0_ffn_bq"].reshape(1, 2048),
        "bk_Br": inputs["L0_ffn_bk"].reshape(1, 2048),
        "bv_Bcol": inputs["L0_ffn_bv"].reshape(16, 128).T,
        "w_ln1": inputs["L0_ffn_w_ln1"], "w_ln2": inputs["L0_ffn_w_ln2"],
        "ident": consts["ident"], "identf": np.eye(128, dtype=np.float32),
        "onesr": np.ones((1, 128), dtype=np.float32), "cmask": consts["cmask"],
        "ropeA": consts["ropeA"], "ropeB": consts["ropeB"],
    }
    base = {k: np.ascontiguousarray(v) for k, v in base.items()}
    in_maps = []
    for c in range(8):
        m = dict(base)
        m["x"] = np.ascontiguousarray(xs[c * BC:(c + 1) * BC])
        in_maps.append(m)
    nc = build_program()
    res = run_bass_kernel_spmd(nc, in_maps, core_ids=list(range(8)))
    out = np.concatenate([r["out"] for r in res.results], axis=0)
    return np.ascontiguousarray(out.astype(np.float32))
